# version 08
# baseline (speedup 1.0000x reference)
import contextlib

import numpy as np
import ml_dtypes

import concourse.bass as bass
import concourse.mybir as mybir
from concourse.bass_utils import run_bass_kernel_spmd
from concourse.ordered_engine_set import OrderedEngineSet

F32 = mybir.dt.float32
BF16 = mybir.dt.bfloat16
ET = mybir.EngineType
AF = mybir.ActivationFunctionType
ALU = mybir.AluOpType
AX = mybir.AxisListType

NCORES = 8
T = 16384
D = 1024
NH = 8
DLAT = 32
HD = 128
FF = 4096
TOWN = 2048
QSCALE = float(HD) ** -0.5
NGRP = T // 512
VB = 66
VG, AG, PG = 5, 3, 3


def build_program():
    nc = bass.Bass("TRN2", target_bir_lowering=False,
                   detect_race_conditions=False)

    x_full = nc.dram_tensor("x_full", [T, D], F32, kind="ExternalInput").ap()
    x_own = nc.dram_tensor("x_own", [TOWN, D], F32, kind="ExternalInput").ap()
    wq = nc.dram_tensor("wq", [D, D], F32, kind="ExternalInput").ap()
    wkv = nc.dram_tensor("wkv", [D, 128], F32, kind="ExternalInput").ap()
    wk_up = nc.dram_tensor("wk_up", [DLAT, HD], F32, kind="ExternalInput").ap()
    wv_up = nc.dram_tensor("wv_up", [DLAT, HD], F32, kind="ExternalInput").ap()
    wo = nc.dram_tensor("wo", [D, D], F32, kind="ExternalInput").ap()
    w_fc = nc.dram_tensor("w_fc", [FF, D], F32, kind="ExternalInput").ap()
    w_proj = nc.dram_tensor("w_proj", [D, FF], F32, kind="ExternalInput").ap()
    masks_in = nc.dram_tensor("masks", [128, 4 * 512], BF16, kind="ExternalInput").ap()
    ident_in = nc.dram_tensor("ident", [128, 128], F32, kind="ExternalInput").ap()
    out = nc.dram_tensor("out", [TOWN, D], F32, kind="ExternalOutput").ap()

    k_dram = nc.dram_tensor("k_dram", [2 * DLAT, T], BF16, kind="Internal").ap()
    y_dram = nc.dram_tensor("y_dram", [128, NH * TOWN], BF16, kind="Internal").ap()
    v_dram = nc.dram_tensor("v_dram", [128, (T // 128) * VB], BF16, kind="Internal").ap()

    with contextlib.ExitStack() as st:
        E = st.enter_context

        xnT_own = E(nc.sbuf_tensor("xnT_own", [128, 8 * TOWN], BF16))
        qpT = E(nc.sbuf_tensor("qpT", [128, 4 * TOWN], BF16))
        wkv_bf = E(nc.sbuf_tensor("wkv_bf", [128, 8 * 128], BF16))
        wvup_bf = E(nc.sbuf_tensor("wvup_bf", [DLAT, HD], BF16))
        wqp = E(nc.sbuf_tensor("wqp", [128, NH * 8 * DLAT], BF16))
        ident_f = E(nc.sbuf_tensor("ident_f", [128, 128], F32))
        ident_b = E(nc.sbuf_tensor("ident_b", [128, 128], BF16))
        ones64w = E(nc.sbuf_tensor("ones64w", [64, 128], BF16))
        ones32w = E(nc.sbuf_tensor("ones32w", [DLAT, 128], BF16))
        zro33 = E(nc.sbuf_tensor("zro33", [64, 33], BF16))
        zro512 = E(nc.sbuf_tensor("zro512", [64, 512], BF16))
        negB = E(nc.sbuf_tensor("negB", [128, 1], F32))
        maxk = E(nc.sbuf_tensor("maxk", [128, NGRP], F32))
        maxq = E(nc.sbuf_tensor("maxq", [128, 32], F32))

        def sem(name):
            return E(nc.semaphore(name))

        g0 = sem("g0"); gm = sem("gm"); gq = sem("gq"); p0 = sem("p0"); v0 = sem("v0")
        gxp = [sem("gxa"), sem("gxb")]
        sv = sem("sv"); sa = sem("sa"); sp = sem("sp")
        sok = sem("sok"); sov = sem("sov")
        mski = sem("mski"); mskm = sem("mskm")
        gx2p = [sem("gx2a"), sem("gx2b")]
        sv2 = sem("sv2"); sa2 = sem("sa2"); sp2 = sem("sp2")
        sq2p = sem("sq2p"); sq2a = sem("sq2a")
        b0 = sem("b0"); ms = sem("ms")
        sdon = sem("sdon"); edon = sem("edon"); pcons = sem("pcons")
        mdon = sem("mdon"); fdon = sem("fdon")
        ktv = sem("ktv"); ydma = sem("ydma")
        bar = sem("bar"); bar2 = sem("bar2")
        gwo = sem("gwo"); gwp = [sem("gwa"), sem("gwb")]
        pw = sem("pw"); vw = sem("vw")
        pfc = sem("pfc"); ar = sem("ar"); vsq = sem("vsq")
        ppr = sem("ppr"); pwo = sem("pwo"); svk = sem("svk"); vres = sem("vres")
        sx = sem("sx"); stt = sem("stt"); syl = sem("syl")

        with contextlib.ExitStack() as skv:
            Ekv = skv.enter_context
            k_sb = Ekv(nc.sbuf_tensor("k_sb", [128, T], BF16))
            v_sb = Ekv(nc.sbuf_tensor("v_sb", [128, (T // 128) * VB], BF16))

            with contextlib.ExitStack() as s1:
                E1 = s1.enter_context
                x_bf = [E1(nc.sbuf_tensor(f"x_bf{i}", [128, 4 * D], F32))
                        for i in range(2)]
                xn_sb = E1(nc.sbuf_tensor("xn_sb", [128, 4 * D], BF16))
                xsq_f = E1(nc.sbuf_tensor("xsq_f", [128, 4 * D], F32))
                stat4 = E1(nc.sbuf_tensor("stat4", [128, 512], F32))
                cfA = E1(nc.sbuf_tensor("cfA", [128, 512], F32))
                sq_in = E1(nc.sbuf_tensor("sq_in", [128, 512], F32))
                xnT_g = E1(nc.sbuf_tensor("xnT_g", [128, 4096], BF16))
                kv_ev = E1(nc.sbuf_tensor("kv_ev", [128, 512], BF16))
                sqk = E1(nc.sbuf_tensor("sqk", [64, 512], BF16))
                sq_q = E1(nc.sbuf_tensor("sq_q", [DLAT, 512], BF16))
                wq_bf = E1(nc.sbuf_tensor("wq_bf", [128, 1024], BF16))
                wqT_b = E1(nc.sbuf_tensor("wqT_b", [128, 128], BF16))
                wkupT = E1(nc.sbuf_tensor("wkupT", [HD, DLAT], BF16))
                wkup_b = E1(nc.sbuf_tensor("wkup_b", [DLAT, HD], BF16))
                bsc = E1(nc.sbuf_tensor("bsc", [128, 8], F32))

                psum_tr = E1(nc.psum_tensor("psum_tr", [128, 4096], BF16))
                psum_kv = E1(nc.psum_tensor("psum_kv", [128, 512], F32))
                psum_nrm = E1(nc.psum_tensor("psum_nrm", [128, 512], F32))
                psum_q = E1(nc.psum_tensor("psum_q", [DLAT, 512], F32))

                blk1 = nc.Block()
                blk1.__enter__()

                @blk1.gpsimd
                def _(g):
                    for k in range(8):
                        g.dma_start(out=wkv_bf[:, k * 128:(k + 1) * 128],
                                    in_=wkv[k * 128:(k + 1) * 128, :]).then_inc(g0, 16)
                    g.dma_start(out=wvup_bf[:], in_=wv_up).then_inc(g0, 16)
                    g.dma_start(out=wkup_b[:], in_=wk_up).then_inc(g0, 16)
                    g.dma_start(out=ident_b[:], in_=ident_in).then_inc(g0, 16)
                    g.memset(ones64w[:], 1.0)
                    g.memset(ones32w[:], 1.0)
                    g.memset(zro33[:], 0.0)
                    g.memset(zro512[:], 0.0)
                    g.memset(stat4[:], 1.0)
                    g.memset(sq_in[:], 1.0)
                    g.memset(v_sb[:], 1.0).then_inc(gm, 1)
                    for k in range(8):
                        if k > 0:
                            g.wait_ge(p0, 2 + 16 * k)
                        g.dma_start(out=wq_bf[:],
                                    in_=wq[k * 128:(k + 1) * 128, :]).then_inc(gq, 16)

                @blk1.tensor
                def _(t):
                    t.wait_ge(g0, 176)
                    t.wait_ge(gm, 1)
                    t.transpose(psum_tr[:, 0:DLAT], wkup_b[:],
                                ident_b[0:DLAT, 0:DLAT]).then_inc(p0, 1)
                    t.wait_ge(v0, 1)
                    t.nop().then_inc(p0, 1)
                    for k in range(8):
                        t.wait_ge(gq, 16 * (k + 1))
                        for h in range(NH):
                            e = 2 + 16 * k + 2 * h
                            t.wait_ge(v0, 2 * (8 * k + h))
                            t.transpose(psum_tr[:, 0:128], wq_bf[:, h * 128:(h + 1) * 128],
                                        ident_b[:]).then_inc(p0, 1)
                            t.wait_ge(v0, 2 * (8 * k + h) + 2)
                            t.matmul(psum_kv[:, 0:DLAT], wqT_b[:], wkupT[:],
                                     start=True, stop=True).then_inc(p0, 1)
                    t.wait_ge(v0, 129)
                    for gi in range(NGRP):
                        t.wait_ge(sv, VG * gi + 2)
                        if gi > 0:
                            t.wait_ge(sv, VG * (gi - 1) + 5)
                        mm = None
                        for i in range(8):
                            for b in range(4):
                                k = 2 * b + (i // 4)
                                tt = i % 4
                                mm = t.transpose(
                                    psum_tr[:, k * 512 + tt * 128:k * 512 + tt * 128 + 128],
                                    xn_sb[:, tt * D + k * 128:tt * D + k * 128 + 128],
                                    ident_b[:])
                        mm.then_inc(sp, 1)
                        t.wait_ge(sv, VG * gi + 3)
                        if gi > 0:
                            t.wait_ge(sv, VG * (gi - 1) + 4)
                        for k in range(8):
                            mm = t.matmul(psum_kv[:], wkv_bf[:, k * 128:(k + 1) * 128],
                                          xnT_g[:, k * 512:(k + 1) * 512],
                                          start=(k == 0), stop=(k == 7))
                        mm.then_inc(sp, 1)
                        t.wait_ge(sv, VG * gi + 4)
                        t.wait_ge(sa, AG * gi + 3)
                        if gi > 0:
                            t.wait_ge(sv, VG * (gi - 1) + 5)
                        for tt in range(4):
                            t.transpose(psum_tr[:, tt * 1024:tt * 1024 + 64],
                                        kv_ev[64:128, tt * 128:(tt + 1) * 128],
                                        ident_b[64:128, 64:128])
                        t.matmul(psum_nrm[:], ones64w[:], sqk[:], start=True,
                                 stop=True).then_inc(sp, 1)

                @blk1.vector
                def _(v):
                    v.wait_ge(p0, 1)
                    v.tensor_copy(out=wkupT[:], in_=psum_tr[:, 0:DLAT])
                    v.drain().then_inc(v0, 1)
                    for k in range(8):
                        for h in range(NH):
                            e = 2 + 16 * k + 2 * h
                            v.wait_ge(p0, e + 1)
                            v.tensor_copy(out=wqT_b[:], in_=psum_tr[:, 0:128])
                            v.drain().then_inc(v0, 1)
                            v.wait_ge(p0, e + 2)
                            v.tensor_scalar(
                                out=wqp[:, (h * 8 + k) * DLAT:(h * 8 + k + 1) * DLAT],
                                in0=psum_kv[:, 0:DLAT], scalar1=QSCALE, scalar2=None,
                                op0=ALU.mult)
                            v.drain().then_inc(v0, 1)
                    def ln_stats_a(v, xb):
                        for tt in range(4):
                            for h2 in range(2):
                                c0 = tt * D + h2 * 512
                                v.reduce_sum(out=stat4[:, 2 * tt + h2:2 * tt + h2 + 1],
                                             in_=xb[:, c0:c0 + 512], axis=AX.X)

                    def ln_stats_b(v):
                        for tt in range(4):
                            for h2 in range(2):
                                c0 = tt * D + h2 * 512
                                v.reduce_sum(out=stat4[:, 8 + 2 * tt + h2:9 + 2 * tt + h2],
                                             in_=xsq_f[:, c0:c0 + 512], axis=AX.X)
                        v.drain()
                        for tt in range(4):
                            v.tensor_tensor(out=stat4[:, 16 + tt:17 + tt],
                                            in0=stat4[:, 2 * tt:2 * tt + 1],
                                            in1=stat4[:, 2 * tt + 1:2 * tt + 2],
                                            op=ALU.add)
                            v.tensor_tensor(out=stat4[:, 20 + tt:21 + tt],
                                            in0=stat4[:, 8 + 2 * tt:9 + 2 * tt],
                                            in1=stat4[:, 9 + 2 * tt:10 + 2 * tt],
                                            op=ALU.add)
                        v.drain()
                        v.tensor_scalar(out=stat4[:, 12:16], in0=stat4[:, 16:20],
                                        scalar1=1.0 / 1024.0, scalar2=None,
                                        op0=ALU.mult)
                        v.tensor_scalar(out=stat4[:, 24:28], in0=stat4[:, 20:24],
                                        scalar1=1.0 / 1024.0, scalar2=None,
                                        op0=ALU.mult)
                        v.drain()
                        v.tensor_tensor(out=stat4[:, 28:32], in0=stat4[:, 12:16],
                                        in1=stat4[:, 12:16], op=ALU.mult)
                        v.drain()
                        v.tensor_tensor(out=stat4[:, 28:32], in0=stat4[:, 24:28],
                                        in1=stat4[:, 28:32], op=ALU.subtract)
                        v.drain()
                        v.tensor_scalar(out=sq_in[:, 24:28], in0=stat4[:, 28:32],
                                        scalar1=1e-5, scalar2=None, op0=ALU.add)
                        return v.drain()

                    for gi in range(NGRP):
                        v.wait_ge(gxp[gi % 2], 64 * (gi // 2 + 1))
                        if gi > 0:
                            v.wait_ge(sa, AG * (gi - 1) + 2)
                        ln_stats_a(v, x_bf[gi % 2])
                        v.wait_ge(sa, AG * gi + 1)
                        ln_stats_b(v).then_inc(sv, 1)
                        v.wait_ge(sa, AG * gi + 2)
                        v.reciprocal(out=stat4[:, 0:4], in_=cfA[:, 24:28])
                        v.drain()
                        v.scalar_tensor_tensor(out=stat4[:, 16:20],
                                               in0=stat4[:, 12:16], scalar=-1.0,
                                               in1=stat4[:, 0:4],
                                               op0=ALU.mult, op1=ALU.mult)
                        v.drain()
                        if gi > 0:
                            v.wait_ge(sp, PG * (gi - 1) + 1)
                        for tt in range(4):
                            v.tensor_scalar(out=xn_sb[:, tt * D:(tt + 1) * D],
                                            in0=x_bf[gi % 2][:, tt * D:(tt + 1) * D],
                                            scalar1=stat4[:, tt:tt + 1],
                                            scalar2=stat4[:, 16 + tt:17 + tt],
                                            op0=ALU.mult, op1=ALU.add)
                        v.drain().then_inc(sv, 1)
                        v.wait_ge(sp, PG * gi + 1)
                        v.tensor_copy(out=xnT_g[:], in_=psum_tr[:])
                        v.drain().then_inc(sv, 1)
                        v.wait_ge(sp, PG * gi + 2)
                        if gi > 0:
                            v.wait_ge(sok, 16 * gi)
                            v.wait_ge(sa, AG * (gi - 1) + 3)
                            v.wait_ge(sp, PG * (gi - 1) + 3)
                        v.tensor_copy(out=kv_ev[:], in_=psum_kv[:])
                        v.tensor_copy(out=k_sb[0:32, 512 * gi:512 * (gi + 1)],
                                      in_=psum_kv[0:32, :])
                        v.tensor_copy(out=k_sb[64:96, 512 * gi:512 * (gi + 1)],
                                      in_=psum_kv[32:64, :])
                        v.drain().then_inc(sv, 1)
                        v.wait_ge(sp, PG * gi + 3)
                        for tt in range(4):
                            vb0 = 4 * VB * gi + tt * VB
                            v.tensor_copy(out=v_sb[:, vb0:vb0 + 32],
                                          in_=psum_tr[:, tt * 1024:tt * 1024 + 32])
                            v.tensor_copy(out=v_sb[:, vb0 + 33:vb0 + 65],
                                          in_=psum_tr[:, tt * 1024 + 32:tt * 1024 + 64])
                        v.drain()
                        v.reduce_max(out=maxk[:, gi:gi + 1], in_=psum_nrm[:],
                                     axis=AX.X)
                        v.drain().then_inc(sv, 1)

                @blk1.scalar
                def _(sc):
                    for gi in range(NGRP):
                        sc.wait_ge(gxp[gi % 2], 64 * (gi // 2 + 1))
                        if gi > 0:
                            sc.wait_ge(sv, VG * (gi - 1) + 1)
                        for q4 in range(4):
                            sc.activation(out=xsq_f[:, q4 * D:(q4 + 1) * D],
                                          in_=x_bf[gi % 2][:, q4 * D:(q4 + 1) * D],
                                          func=AF.Square)
                        sc.drain().then_inc(sa, 1)
                        sc.wait_ge(sv, VG * gi + 1)
                        sc.activation(out=cfA[:], in_=sq_in[:], func=AF.Sqrt,
                                      bias=0.0, scale=1.0)
                        sc.drain().then_inc(sa, 1)
                        sc.wait_ge(sv, VG * gi + 4)
                        if gi > 0:
                            sc.wait_ge(sp, PG * (gi - 1) + 3)
                        sc.activation(out=sqk[:], in_=kv_ev[0:64, :],
                                      func=AF.Square)
                        sc.drain().then_inc(sa, 1)

                @blk1.sync
                def _(sy):
                    sy.dma_start(out=ident_f[:], in_=ident_in).then_inc(mski, 16)
                    for gi in range(NGRP):
                        if gi >= 2:
                            sy.wait_ge(sv, VG * (gi - 2) + 2)
                            sy.wait_ge(sa, AG * (gi - 2) + 1)
                        for n in range(4):
                            r0 = 512 * gi + n * 128
                            sy.dma_start(out=x_bf[gi % 2][:, n * D:(n + 1) * D],
                                         in_=x_full[r0:r0 + 128, :]
                                         ).then_inc(gxp[gi % 2], 16)
                        if gi >= 1:
                            sy.wait_ge(sv, VG * (gi - 1) + 4)
                            sy.dma_start(out=k_dram[:, 512 * (gi - 1):512 * gi],
                                         in_=kv_ev[0:64, :]).then_inc(sok, 16)
                            sy.wait_ge(sv, VG * (gi - 1) + 5)
                            sy.dma_start(out=v_dram[:, 4 * VB * (gi - 1):4 * VB * gi],
                                         in_=v_sb[:, 4 * VB * (gi - 1):4 * VB * gi]
                                         ).then_inc(sov, 16)
                    gl = NGRP - 1
                    sy.wait_ge(sv, VG * gl + 4)
                    sy.dma_start(out=k_dram[:, 512 * gl:512 * (gl + 1)],
                                 in_=kv_ev[0:64, :]).then_inc(sok, 16)
                    sy.wait_ge(sv, VG * gl + 5)
                    sy.dma_start(out=v_dram[:, 4 * VB * gl:4 * VB * (gl + 1)],
                                 in_=v_sb[:, 4 * VB * gl:4 * VB * (gl + 1)]
                                 ).then_inc(sov, 16)

                blk1.__exit__(None, None, None)

                blk2 = nc.Block()
                blk2.__enter__()

                @blk2.sync
                def _(sy):
                    for gi in range(4):
                        if gi < 2:
                            sy.wait_ge(sv, VG * (NGRP - 1) + 2)
                            sy.wait_ge(sa, AG * (NGRP - 1) + 1)
                        else:
                            sy.wait_ge(sv2, 3 * (gi - 2) + 2)
                            sy.wait_ge(sa2, 2 * (gi - 2) + 1)
                        for n in range(4):
                            r0 = 512 * gi + n * 128
                            sy.dma_start(out=x_bf[gi % 2][:, n * D:(n + 1) * D],
                                         in_=x_own[r0:r0 + 128, :]
                                         ).then_inc(gx2p[gi % 2], 16)

                @blk2.vector
                def _(v):
                    for gi in range(4):
                        v.wait_ge(gx2p[gi % 2], 64 * (gi // 2 + 1))
                        if gi == 0:
                            v.wait_ge(sa, AG * (NGRP - 1) + 2)
                        else:
                            v.wait_ge(sa2, 2 * (gi - 1) + 2)
                        xb2 = x_bf[gi % 2]
                        for tt in range(4):
                            for h2 in range(2):
                                c0 = tt * D + h2 * 512
                                v.reduce_sum(out=stat4[:, 2 * tt + h2:2 * tt + h2 + 1],
                                             in_=xb2[:, c0:c0 + 512], axis=AX.X)
                        v.wait_ge(sa2, 2 * gi + 1)
                        for tt in range(4):
                            for h2 in range(2):
                                c0 = tt * D + h2 * 512
                                v.reduce_sum(out=stat4[:, 8 + 2 * tt + h2:9 + 2 * tt + h2],
                                             in_=xsq_f[:, c0:c0 + 512], axis=AX.X)
                        v.drain()
                        for tt in range(4):
                            v.tensor_tensor(out=stat4[:, 16 + tt:17 + tt],
                                            in0=stat4[:, 2 * tt:2 * tt + 1],
                                            in1=stat4[:, 2 * tt + 1:2 * tt + 2],
                                            op=ALU.add)
                            v.tensor_tensor(out=stat4[:, 20 + tt:21 + tt],
                                            in0=stat4[:, 8 + 2 * tt:9 + 2 * tt],
                                            in1=stat4[:, 9 + 2 * tt:10 + 2 * tt],
                                            op=ALU.add)
                        v.drain()
                        v.tensor_scalar(out=stat4[:, 12:16], in0=stat4[:, 16:20],
                                        scalar1=1.0 / 1024.0, scalar2=None,
                                        op0=ALU.mult)
                        v.tensor_scalar(out=stat4[:, 24:28], in0=stat4[:, 20:24],
                                        scalar1=1.0 / 1024.0, scalar2=None,
                                        op0=ALU.mult)
                        v.drain()
                        v.tensor_tensor(out=stat4[:, 28:32], in0=stat4[:, 12:16],
                                        in1=stat4[:, 12:16], op=ALU.mult)
                        v.drain()
                        v.tensor_tensor(out=stat4[:, 28:32], in0=stat4[:, 24:28],
                                        in1=stat4[:, 28:32], op=ALU.subtract)
                        v.drain()
                        v.tensor_scalar(out=sq_in[:, 24:28], in0=stat4[:, 28:32],
                                        scalar1=1e-5, scalar2=None, op0=ALU.add)
                        v.drain().then_inc(sv2, 1)
                        v.wait_ge(sa2, 2 * gi + 2)
                        v.reciprocal(out=stat4[:, 0:4], in_=cfA[:, 24:28])
                        v.drain()
                        v.scalar_tensor_tensor(out=stat4[:, 16:20],
                                               in0=stat4[:, 12:16], scalar=-1.0,
                                               in1=stat4[:, 0:4],
                                               op0=ALU.mult, op1=ALU.mult)
                        v.drain()
                        if gi == 0:
                            v.wait_ge(sp, PG * (NGRP - 1) + 1)
                        else:
                            v.wait_ge(sp2, gi)
                        for tt in range(4):
                            v.tensor_scalar(out=xn_sb[:, tt * D:(tt + 1) * D],
                                            in0=x_bf[gi % 2][:, tt * D:(tt + 1) * D],
                                            scalar1=stat4[:, tt:tt + 1],
                                            scalar2=stat4[:, 16 + tt:17 + tt],
                                            op0=ALU.mult, op1=ALU.add)
                        v.drain().then_inc(sv2, 1)
                        v.wait_ge(sp2, gi + 1)
                        for k in range(8):
                            v.tensor_copy(
                                out=xnT_own[:, k * TOWN + gi * 512:k * TOWN + (gi + 1) * 512],
                                in_=psum_tr[:, k * 512:(k + 1) * 512])
                        v.drain().then_inc(sv2, 1)

                @blk2.scalar
                def _(sc):
                    for gi in range(4):
                        sc.wait_ge(gx2p[gi % 2], 64 * (gi // 2 + 1))
                        if gi > 0:
                            sc.wait_ge(sv2, 3 * (gi - 1) + 1)
                        for q4 in range(4):
                            sc.activation(out=xsq_f[:, q4 * D:(q4 + 1) * D],
                                          in_=x_bf[gi % 2][:, q4 * D:(q4 + 1) * D],
                                          func=AF.Square)
                        sc.drain().then_inc(sa2, 1)
                        sc.wait_ge(sv2, 3 * gi + 1)
                        sc.activation(out=cfA[:], in_=sq_in[:], func=AF.Sqrt,
                                      bias=0.0, scale=1.0)
                        sc.drain().then_inc(sa2, 1)
                        for h in range(NH):
                            sc.wait_ge(sq2p, 8 * gi + h + 1)
                            qb = 64 * (h // 4)
                            qc0 = (h % 4) * TOWN + gi * 512
                            sc.copy(out=qpT[qb:qb + DLAT, qc0:qc0 + 512],
                                    in_=psum_q[:])
                            sc.drain().then_inc(sq2a, 1)

                @blk2.tensor
                def _(t):
                    for gi in range(4):
                        t.wait_ge(sv2, 3 * gi + 2)
                        if gi == 0:
                            t.wait_ge(sv, VG * (NGRP - 1) + 5)
                        else:
                            t.wait_ge(sv2, 3 * (gi - 1) + 3)
                        mm = None
                        for i in range(8):
                            for b in range(4):
                                k = 2 * b + (i // 4)
                                tt = i % 4
                                mm = t.transpose(
                                    psum_tr[:, k * 512 + tt * 128:k * 512 + tt * 128 + 128],
                                    xn_sb[:, tt * D + k * 128:tt * D + k * 128 + 128],
                                    ident_b[:])
                        mm.then_inc(sp2, 1)
                        t.wait_ge(sv2, 3 * gi + 3)
                        for h in range(NH):
                            t.wait_ge(sq2a, 8 * gi + h)
                            for k in range(8):
                                mm = t.matmul(
                                    psum_q[:],
                                    wqp[:, (h * 8 + k) * DLAT:(h * 8 + k + 1) * DLAT],
                                    xnT_own[:, k * TOWN + gi * 512:k * TOWN + (gi + 1) * 512],
                                    start=(k == 0), stop=(k == 7))
                            mm.then_inc(sq2p, 1)

                blk2.__exit__(None, None, None)

                blk3 = nc.Block()
                blk3.__enter__()

                @blk3.scalar
                def _(sc):
                    for ch in range(32):
                        h, gi4 = ch // 4, ch % 4
                        qb = 64 * (h // 4)
                        qc0 = (h % 4) * TOWN + gi4 * 512
                        sc.wait_ge(b0, 3 * ch)
                        sc.activation(out=sq_q[:], in_=qpT[qb:qb + DLAT, qc0:qc0 + 512],
                                      func=AF.Square)
                        sc.drain().then_inc(b0, 1)
                    sc.wait_ge(b0, 97)
                    sc.activation(out=cfA[:], in_=sq_in[:], func=AF.Sqrt,
                                  bias=0.0, scale=1.0)
                    sc.drain().then_inc(b0, 1)

                @blk3.tensor
                def _(t):
                    t.wait_ge(sv, VG * NGRP)
                    for ch in range(32):
                        t.wait_ge(b0, 3 * ch + 1)
                        t.matmul(psum_nrm[:], ones32w[:], sq_q[:],
                                 start=True, stop=True).then_inc(b0, 1)

                @blk3.vector
                def _(v):
                    for ch in range(32):
                        v.wait_ge(b0, 3 * ch + 2)
                        v.reduce_max(out=maxq[:, ch:ch + 1], in_=psum_nrm[:],
                                     axis=AX.X)
                        v.drain().then_inc(b0, 1)
                    v.reduce_max(out=bsc[:, 0:1], in_=maxq[:], axis=AX.X)
                    v.reduce_max(out=bsc[:, 1:2], in_=maxk[:], axis=AX.X)
                    v.drain()
                    v.tensor_tensor(out=bsc[:, 2:3], in0=bsc[:, 0:1], in1=bsc[:, 1:2],
                                    op=ALU.mult)
                    v.drain()
                    v.tensor_scalar(out=sq_in[:, 24:25], in0=bsc[:, 2:3],
                                    scalar1=1e-30, scalar2=None, op0=ALU.add)
                    v.drain().then_inc(b0, 1)
                    v.wait_ge(b0, 98)
                    v.tensor_scalar(out=negB[:], in0=cfA[:, 24:25], scalar1=-1.0,
                                    scalar2=None, op0=ALU.mult)
                    v.drain().then_inc(ms, 1)

                blk3.__exit__(None, None, None)

            with contextlib.ExitStack() as s2:
                E2 = s2.enter_context
                ktail = E2(nc.sbuf_tensor("ktail", [128, 2048], BF16))
                vtail = E2(nc.sbuf_tensor("vtail", [128, 2 * 8 * VB], BF16))
                pt1 = E2(nc.sbuf_tensor("pt1", [128, 2048], BF16))
                pt2 = E2(nc.sbuf_tensor("pt2", [128, 2048], BF16))
                pm = E2(nc.sbuf_tensor("pm", [128, 2048], BF16))
                olat_sb = E2(nc.sbuf_tensor("olat_sb", [33, 512], F32))
                onorm = E2(nc.sbuf_tensor("onorm", [128, 128], BF16))
                rec4 = E2(nc.sbuf_tensor("rec4", [128, 4], F32))
                olatTn = E2(nc.sbuf_tensor("olatTn", [DLAT, 512], BF16))
                y_stage = E2(nc.sbuf_tensor("y_stage", [128, 512], BF16))
                masks = E2(nc.sbuf_tensor("masks_sb", [128, 4 * 512], BF16))

                psum_s = E2(nc.psum_tensor("psum_s", [128, 2048], F32))
                psum_o = E2(nc.psum_tensor("psum_o", [33, 512], F32))
                psum_f2 = [E2(nc.psum_tensor(f"psum_f2{i}", [DLAT, 256], BF16))
                           for i in range(2)]
                psum_yb = E2(nc.psum_tensor("psum_yb", [128, 512], F32))

                qtiles = [(h, blk, qt) for h in range(NH) for blk in range(2)
                          for qt in range(2)]
                PB_SEMS = [sdon, edon, mdon, fdon]

                def skipchain(eng, et, rj, ri, maxj, body_fn, tag):
                    engines = OrderedEngineSet([et])
                    for b in range(maxj):
                        lid = nc.next_id()
                        l0, l1 = f"{tag}{lid}_b", f"{tag}{lid}_e"
                        eng.reg_mov(ri, b)
                        eng.br_cmp(ri, rj, on_true=l0, on_false=l1,
                                   comp_op="IS_LT")
                        with nc.body(l0, valid_engines=engines):
                            body_fn(b)
                            eng.reg_alu(ri, ri, 64, op=ALU.add)
                            eng.br_cmp(ri, rj, on_true=l0, on_false=l1,
                                       comp_op="IS_LT")
                        nc.switch_bb(l1)

                def wge(eng, sem_h, base_reg, mul, add, rt):
                    eng.reg_alu(rt, base_reg, mul, op=ALU.mult)
                    if add:
                        eng.reg_alu(rt, rt, add, op=ALU.add)
                    eng.wait_ge(sem_h, rt)

                blkb = nc.Block()
                blkb.__enter__()

                @blkb.sync
                def _(sy):
                    sy.dma_start(out=masks[:], in_=masks_in).then_inc(mskm, 16)
                    pid_sp = nc.partition_id([ET.SP])
                    rB = sy.alloc_register("sy_rB")
                    rt0 = sy.alloc_register("sy_rt0")
                    sy.reg_mov(rt0, 15)
                    sy.reg_alu(rB, rt0, pid_sp, op=ALU.subtract)
                    jB = nc.snap(rB, min_val=8, max_val=15)
                    jA = nc.snap(pid_sp, min_val=0, max_val=7)
                    sy.wait_ge(ms, 1)
                    for blk, jj in ((0, jA), (1, jB)):
                        for g in range(2):
                            sy.dma_start(
                                out=ktail[64 * g:64 * g + DLAT,
                                          blk * 1024:(blk + 1) * 1024],
                                in_=k_dram[32 * g:32 * g + DLAT, bass.ts(jj, 1024)]
                                ).then_inc(ktv, 16)
                        sy.dma_start(out=vtail[:, blk * 8 * VB:(blk + 1) * 8 * VB],
                                     in_=v_dram[:, bass.ts(jj, 8 * VB)]
                                     ).then_inc(ktv, 16)
                    for qi, (h, blk, qt) in enumerate(qtiles):
                        qcol = h * TOWN + blk * 1024 + qt * 512
                        if qi > 0:
                            sy.wait_ge(bar2, qi)
                        sy.wait_ge(fdon, 8)
                        sy.dma_start(out=y_dram[:, qcol:qcol + 512],
                                     in_=y_stage[:]).then_inc(ydma, 16)
                        sy.wait_ge(ydma, 16 * (qi + 1))
                        sy.nop().then_inc(bar, 1)

                @blkb.tensor
                def _(t):
                    pid_pe = nc.partition_id([ET.PE])
                    rjA = t.alloc_register("pe_rjA")
                    rjB = t.alloc_register("pe_rjB")
                    rt = t.alloc_register("pe_rt")
                    ri = t.alloc_register("pe_ri")
                    t.reg_alu(rjA, pid_pe, 0, op=ALU.add)
                    t.reg_mov(rt, 15)
                    t.reg_alu(rjB, rt, pid_pe, op=ALU.subtract)
                    t.wait_ge(ms, 1)
                    t.wait_ge(mski, 16)
                    t.wait_ge(ktv, 96)
                    for qi, (h, blk, qt) in enumerate(qtiles):
                        g = h // 4
                        qb = 64 * g
                        qpcol = (h % 4) * TOWN + blk * 1024 + qt * 512
                        rj = rjA if blk == 0 else rjB
                        maxj = 7 if blk == 0 else 15
                        if qi > 0:
                            t.wait_ge(bar2, qi)
                        t.matmul(psum_o[:], zro33[:], zro512[:], start=True,
                                 stop=False)

                        def pe_body(b):
                            t.wait_ge(edon, 2 * b)
                            mm = None
                            for w in range(4):
                                mm = t.matmul(psum_s[:, w * 512:(w + 1) * 512],
                                              k_sb[qb:qb + DLAT,
                                                   (8 * b + w) * 128:(8 * b + w + 1) * 128],
                                              qpT[qb:qb + DLAT, qpcol:qpcol + 512],
                                              start=True, stop=True)
                            mm.then_inc(sdon, 1)
                            t.wait_ge(edon, 2 * b + 1)
                            for w in range(4, 8):
                                mm = t.matmul(psum_s[:, (w - 4) * 512:(w - 3) * 512],
                                              k_sb[qb:qb + DLAT,
                                                   (8 * b + w) * 128:(8 * b + w + 1) * 128],
                                              qpT[qb:qb + DLAT, qpcol:qpcol + 512],
                                              start=True, stop=True)
                            mm.then_inc(sdon, 1)
                            for w in range(4):
                                mm = t.matmul(
                                    psum_o[:],
                                    v_sb[:, (8 * b + w) * VB + 33 * g:
                                         (8 * b + w) * VB + 33 * g + 33],
                                    pt1[:, w * 512:(w + 1) * 512],
                                    start=False, stop=False)
                            t.wait_ge(edon, 2 * b + 2)
                            for w in range(4, 8):
                                mm = t.matmul(
                                    psum_o[:],
                                    v_sb[:, (8 * b + w) * VB + 33 * g:
                                         (8 * b + w) * VB + 33 * g + 33],
                                    pt2[:, (w - 4) * 512:(w - 3) * 512],
                                    start=False, stop=False)

                        skipchain(t, ET.PE, rj, ri, maxj, pe_body, "pe")

                        kt0 = blk * 1024
                        vt0 = blk * 8 * VB
                        wge(t, edon, rj, 2, 0, rt)
                        if qt == 0:
                            mm = None
                            for w in range(4):
                                mm = t.matmul(psum_s[:, w * 512:(w + 1) * 512],
                                              ktail[qb:qb + DLAT,
                                                    kt0 + w * 128:kt0 + (w + 1) * 128],
                                              qpT[qb:qb + DLAT, qpcol:qpcol + 512],
                                              start=True, stop=True)
                            mm.then_inc(sdon, 1)
                            t.wait_ge(mdon, 1)
                            for w in range(4):
                                mm = t.matmul(
                                    psum_o[:],
                                    vtail[:, vt0 + w * VB + 33 * g:
                                          vt0 + w * VB + 33 * g + 33],
                                    pm[:, w * 512:(w + 1) * 512],
                                    start=False, stop=(w == 3))
                            mm.then_inc(fdon, 1)
                        else:
                            mm = None
                            for w in range(4):
                                mm = t.matmul(psum_s[:, w * 512:(w + 1) * 512],
                                              ktail[qb:qb + DLAT,
                                                    kt0 + w * 128:kt0 + (w + 1) * 128],
                                              qpT[qb:qb + DLAT, qpcol:qpcol + 512],
                                              start=True, stop=True)
                            mm.then_inc(sdon, 1)
                            wge(t, edon, rj, 2, 1, rt)
                            for w in range(4, 8):
                                mm = t.matmul(psum_s[:, (w - 4) * 512:(w - 3) * 512],
                                              ktail[qb:qb + DLAT,
                                                    kt0 + w * 128:kt0 + (w + 1) * 128],
                                              qpT[qb:qb + DLAT, qpcol:qpcol + 512],
                                              start=True, stop=True)
                            mm.then_inc(sdon, 1)
                            for w in range(4):
                                t.matmul(psum_o[:],
                                         vtail[:, vt0 + w * VB + 33 * g:
                                               vt0 + w * VB + 33 * g + 33],
                                         pt1[:, w * 512:(w + 1) * 512],
                                         start=False, stop=False)
                            t.wait_ge(mdon, 1)
                            for w in range(4, 8):
                                mm = t.matmul(
                                    psum_o[:],
                                    vtail[:, vt0 + w * VB + 33 * g:
                                          vt0 + w * VB + 33 * g + 33],
                                    pm[:, (w - 4) * 512:(w - 3) * 512],
                                    start=False, stop=(w == 7))
                            mm.then_inc(fdon, 1)

                        t.wait_ge(fdon, 2)
                        for tt in range(4):
                            mm = t.transpose(psum_s[:, tt * 512:tt * 512 + 33],
                                             olat_sb[:, tt * 128:(tt + 1) * 128],
                                             ident_f[0:33, 0:33])
                        mm.then_inc(fdon, 1)
                        t.wait_ge(fdon, 4)
                        for tt in range(4):
                            mm = t.transpose(
                                psum_f2[tt % 2][:, (tt // 2) * 128:(tt // 2) * 128 + 128],
                                onorm[:, tt * 32:(tt + 1) * 32], ident_b[:])
                        mm.then_inc(fdon, 1)
                        t.wait_ge(fdon, 6)
                        t.matmul(psum_yb[:], wvup_bf[:], olatTn[:], start=True,
                                 stop=True).then_inc(fdon, 1)
                        t.nop().then_inc(bar, 1)

                @blkb.scalar
                def _(sc):
                    pid_ac = nc.partition_id([ET.Activation])
                    rjA = sc.alloc_register("ac_rjA")
                    rjB = sc.alloc_register("ac_rjB")
                    rt = sc.alloc_register("ac_rt")
                    ri = sc.alloc_register("ac_ri")
                    sc.reg_alu(rjA, pid_ac, 0, op=ALU.add)
                    sc.reg_mov(rt, 15)
                    sc.reg_alu(rjB, rt, pid_ac, op=ALU.subtract)
                    sc.wait_ge(ms, 1)
                    for qi, (h, blk, qt) in enumerate(qtiles):
                        rj = rjA if blk == 0 else rjB
                        maxj = 7 if blk == 0 else 15
                        if qi > 0:
                            sc.wait_ge(bar2, qi)

                        def act_body(b):
                            sc.wait_ge(sdon, 2 * b + 1)
                            sc.activation(out=pt1[:], in_=psum_s[:], func=AF.Exp,
                                          bias=negB[:, 0:1], scale=1.0)
                            sc.drain().then_inc(edon, 1)
                            sc.wait_ge(sdon, 2 * b + 2)
                            sc.activation(out=pt2[:], in_=psum_s[:], func=AF.Exp,
                                          bias=negB[:, 0:1], scale=1.0)
                            sc.drain().then_inc(edon, 1)

                        skipchain(sc, ET.Activation, rj, ri, maxj, act_body, "ac")

                        wge(sc, sdon, rj, 2, 1, rt)
                        sc.activation(out=pt1[:], in_=psum_s[:], func=AF.Exp,
                                      bias=negB[:, 0:1], scale=1.0)
                        sc.drain().then_inc(edon, 1)
                        if qt == 1:
                            wge(sc, sdon, rj, 2, 2, rt)
                            sc.activation(out=pt2[:], in_=psum_s[:], func=AF.Exp,
                                          bias=negB[:, 0:1], scale=1.0)
                            sc.drain().then_inc(edon, 1)
                        sc.wait_ge(fdon, 7)
                        sc.copy(out=y_stage[:], in_=psum_yb[:])
                        sc.drain().then_inc(fdon, 1)
                        sc.nop().then_inc(bar, 1)

                @blkb.vector
                def _(v):
                    pid_dv = nc.partition_id([ET.DVE])
                    rjA = v.alloc_register("dv_rjA")
                    rjB = v.alloc_register("dv_rjB")
                    rt = v.alloc_register("dv_rt")
                    v.reg_alu(rjA, pid_dv, 0, op=ALU.add)
                    v.reg_mov(rt, 15)
                    v.reg_alu(rjB, rt, pid_dv, op=ALU.subtract)
                    v.wait_ge(ms, 1)
                    v.wait_ge(mskm, 16)
                    for qi, (h, blk, qt) in enumerate(qtiles):
                        rj = rjA if blk == 0 else rjB
                        if qi > 0:
                            v.wait_ge(bar2, qi)
                        srcb = pt1 if qt == 0 else pt2
                        wge(v, edon, rj, 2, 1 if qt == 0 else 2, rt)
                        for w in range(4):
                            v.tensor_tensor(out=pm[:, w * 512:(w + 1) * 512],
                                            in0=srcb[:, w * 512:(w + 1) * 512],
                                            in1=masks[:, w * 512:(w + 1) * 512],
                                            op=ALU.mult)
                        v.drain().then_inc(mdon, 1)
                        v.wait_ge(fdon, 1)
                        v.tensor_copy(out=olat_sb[:], in_=psum_o[:])
                        v.drain().then_inc(fdon, 1)
                        v.wait_ge(fdon, 3)
                        for tt in range(4):
                            v.reciprocal(out=rec4[:, tt:tt + 1],
                                         in_=psum_s[:, tt * 512 + 32:tt * 512 + 33])
                        v.drain()
                        for tt in range(4):
                            v.tensor_scalar(out=onorm[:, tt * 32:(tt + 1) * 32],
                                            in0=psum_s[:, tt * 512:tt * 512 + 32],
                                            scalar1=rec4[:, tt:tt + 1],
                                            scalar2=None, op0=ALU.mult)
                        v.drain().then_inc(fdon, 1)
                        v.wait_ge(fdon, 5)
                        for tt in range(4):
                            v.tensor_copy(
                                out=olatTn[:, tt * 128:(tt + 1) * 128],
                                in_=psum_f2[tt % 2][:, (tt // 2) * 128:(tt // 2) * 128 + 128])
                        v.drain().then_inc(fdon, 1)
                        v.nop().then_inc(bar, 1)
                        v.wait_ge(bar, 4 * (qi + 1))
                        for s_ in PB_SEMS:
                            v.sem_clear(s_)
                        v.nop().then_inc(bar2, 1)

                blkb.__exit__(None, None, None)

        with contextlib.ExitStack() as s3:
            E3 = s3.enter_context
            wpT = E3(nc.sbuf_tensor("wpT", [128, 32 * 1024], BF16))
            hT = E3(nc.sbuf_tensor("hT", [128, 32 * 512], BF16))
            wo_sb = E3(nc.sbuf_tensor("wo_sb", [128, 8 * 1024], BF16))
            wraw = [E3(nc.sbuf_tensor(f"wraw{i}", [128, 1024], BF16)) for i in range(2)]
            wfT = [E3(nc.sbuf_tensor(f"wfT{i}", [128, 1024], BF16)) for i in range(2)]
            r_bf = [E3(nc.sbuf_tensor(f"r_bf{i}", [128, 512], BF16)) for i in range(2)]
            yf_sb = E3(nc.sbuf_tensor("yf_sb", [128, 1024], F32))
            mem_f = E3(nc.sbuf_tensor("mem_f", [128, 1024], F32))
            spk_f = E3(nc.sbuf_tensor("spk_f", [128, 1024], F32))
            acc_f = E3(nc.sbuf_tensor("acc_f", [128, 1024], F32))
            x_t = E3(nc.sbuf_tensor("x_t", [128, 1024], F32))
            o_t = E3(nc.sbuf_tensor("o_t", [128, 1024], F32))
            y_ld = E3(nc.sbuf_tensor("y_ld", [128, 8 * 128], BF16))

            psum_w = E3(nc.psum_tensor("psum_w", [128, 2048], BF16))
            psum_fc = [E3(nc.psum_tensor(f"psum_fc{i}", [128, 512], F32))
                       for i in range(2)]
            psum_yf = [E3(nc.psum_tensor(f"psum_yf{i}", [128, 1024], F32))
                       for i in range(2)]

            blkc = nc.Block()
            blkc.__enter__()

            @blkc.gpsimd
            def _(g):
                g.wait_ge(bar2, 32)
                for k in range(8):
                    g.dma_start(out=wo_sb[:, k * 1024:(k + 1) * 1024],
                                in_=wo[k * 128:(k + 1) * 128, :]).then_inc(gwo, 16)
                for ph in range(32):
                    if ph >= 2:
                        g.wait_ge(pw, ph - 1)
                    for k in range(8):
                        g.dma_start(out=wraw[ph % 2][:, k * 128:(k + 1) * 128],
                                    in_=w_proj[k * 128:(k + 1) * 128,
                                               ph * 128:(ph + 1) * 128]
                                    ).then_inc(gwp[ph % 2], 16)
                for nn in range(128):
                    ph = nn % 32
                    g.wait_ge(pw, 31 + nn)
                    g.dma_start(out=wraw[nn % 2][:],
                                in_=w_fc[ph * 128:(ph + 1) * 128, :]).then_inc(gwp[nn % 2], 16)

            @blkc.tensor
            def _(t):
                t.wait_ge(bar2, 32)
                for ph in range(32):
                    t.wait_ge(gwp[ph % 2], 128 * (ph // 2 + 1))
                    if ph > 0:
                        t.wait_ge(vw, ph)
                    mm = None
                    for k in range(8):
                        pc = (k % 2) * 1024 + (k // 2) * 128
                        mm = t.transpose(psum_w[:, pc:pc + 128],
                                         wraw[ph % 2][:, k * 128:(k + 1) * 128],
                                         ident_b[:])
                    mm.then_inc(pw, 1)
                for hb in range(4):
                    for ph in range(32):
                        nn = hb * 32 + ph
                        t.wait_ge(gwp[nn % 2], 128 * 16 + 16 * (nn // 2 + 1))
                        t.wait_ge(vw, 32 + nn)
                        mm = None
                        for k in range(8):
                            pc = (k % 2) * 1024 + (k // 2) * 128
                            mm = t.transpose(psum_w[:, pc:pc + 128],
                                             wraw[nn % 2][:, k * 128:(k + 1) * 128],
                                             ident_b[:])
                        mm.then_inc(pw, 1)
                        t.wait_ge(vw, 32 + nn + 1)
                        if nn >= 2:
                            t.wait_ge(ar, nn - 1)
                        for k in range(8):
                            mm = t.matmul(
                                psum_fc[ph % 2][:],
                                wfT[nn % 2][:, k * 128:(k + 1) * 128],
                                xnT_own[:, k * TOWN + hb * 512:k * TOWN + hb * 512 + 512],
                                start=(k == 0), stop=(k == 7))
                        mm.then_inc(pfc, 1)
                    for m in range(4):
                        mt = hb * 4 + m
                        t.wait_ge(vsq, 32 * (hb + 1))
                        if mt >= 2:
                            t.wait_ge(vres, mt - 1)
                        for ph in range(32):
                            for nf in range(2):
                                mm = t.matmul(
                                    psum_yf[m % 2][:, nf * 512:(nf + 1) * 512],
                                    hT[:, ph * 512 + m * 128:ph * 512 + m * 128 + 128],
                                    wpT[:, ph * 1024 + nf * 512:ph * 1024 + (nf + 1) * 512],
                                    start=(ph == 0), stop=(ph == 31))
                        mm.then_inc(ppr, 1)
                        t.wait_ge(svk, mt + 1)
                        t.wait_ge(syl, 16 * (mt + 1))
                        t.wait_ge(gwo, 128)
                        for nf in range(2):
                            for hh in range(NH):
                                mm = t.matmul(
                                    psum_yf[m % 2][:, nf * 512:(nf + 1) * 512],
                                    y_ld[:, hh * 128:(hh + 1) * 128],
                                    wo_sb[:, hh * 1024 + nf * 512:hh * 1024 + (nf + 1) * 512],
                                    start=(hh == 0), stop=(hh == 7))
                        mm.then_inc(pwo, 1)

            @blkc.scalar
            def _(sc):
                sc.wait_ge(bar2, 32)
                for nn in range(128):
                    ph = nn % 32
                    sc.wait_ge(pfc, nn + 1)
                    if nn >= 2:
                        sc.wait_ge(vsq, nn - 1)
                    sc.activation(out=r_bf[ph % 2][:], in_=psum_fc[ph % 2][:],
                                  func=AF.Relu)
                    sc.drain().then_inc(ar, 1)

            @blkc.vector
            def _(v):
                v.wait_ge(bar2, 32)
                for ph in range(32):
                    v.wait_ge(pw, ph + 1)
                    v.tensor_copy(
                        out=wpT[:, ph * 1024:(ph + 1) * 1024].rearrange(
                            "p (r b c) -> p r b c", r=4, b=2),
                        in_=psum_w[:].rearrange("p (b r c) -> p r b c",
                                                b=2, r=8)[:, 0:4, :, :])
                    v.drain().then_inc(vw, 1)
                for hb in range(4):
                    for ph in range(32):
                        nn = hb * 32 + ph
                        v.wait_ge(pw, 32 + nn + 1)
                        if nn >= 2:
                            v.wait_ge(pfc, nn - 1)
                        v.tensor_copy(
                            out=wfT[nn % 2][:].rearrange(
                                "p (r b c) -> p r b c", r=4, b=2),
                            in_=psum_w[:].rearrange("p (b r c) -> p r b c",
                                                    b=2, r=8)[:, 0:4, :, :])
                        v.drain().then_inc(vw, 1)
                        v.wait_ge(ar, nn + 1)
                        if hb > 0:
                            v.wait_ge(ppr, 4 * hb)
                        v.tensor_tensor(out=hT[:, ph * 512:(ph + 1) * 512],
                                        in0=r_bf[ph % 2][:], in1=r_bf[ph % 2][:],
                                        op=ALU.mult)
                        v.drain().then_inc(vsq, 1)
                    for m in range(4):
                        mt = hb * 4 + m
                        v.wait_ge(ppr, mt + 1)
                        v.tensor_copy(out=yf_sb[:], in_=psum_yf[m % 2][:])
                        yp = yf_sb
                        v.drain()
                        v.tensor_copy(out=mem_f[:], in_=yp[:])
                        v.drain()
                        v.tensor_scalar(out=spk_f[:], in0=mem_f[:], scalar1=1.0,
                                        scalar2=None, op0=ALU.is_gt)
                        v.drain()
                        v.tensor_copy(out=acc_f[:], in_=spk_f[:])
                        v.drain()
                        for _step in range(4):
                            v.scalar_tensor_tensor(out=mem_f[:], in0=mem_f[:],
                                                   scalar=0.9, in1=yp[:],
                                                   op0=ALU.mult, op1=ALU.add)
                            v.drain()
                            v.tensor_tensor(out=mem_f[:], in0=mem_f[:], in1=spk_f[:],
                                            op=ALU.subtract)
                            v.drain()
                            v.tensor_scalar(out=spk_f[:], in0=mem_f[:], scalar1=1.0,
                                            scalar2=None, op0=ALU.is_gt)
                            v.drain()
                            v.tensor_tensor(out=acc_f[:], in0=acc_f[:], in1=spk_f[:],
                                            op=ALU.add)
                            v.drain()
                        v.drain().then_inc(svk, 1)
                        v.wait_ge(pwo, mt + 1)
                        v.wait_ge(sx, 16 * (mt + 1))
                        v.tensor_copy(out=yf_sb[:], in_=psum_yf[m % 2][:])
                        v.drain()
                        v.scalar_tensor_tensor(
                            out=mem_f[:], in0=acc_f[:], scalar=0.2,
                            in1=yf_sb[:], op0=ALU.mult, op1=ALU.add)
                        v.drain()
                        if mt >= 1:
                            v.wait_ge(stt, 16 * mt)
                        v.tensor_tensor(out=o_t[:], in0=mem_f[:], in1=x_t[:],
                                        op=ALU.add)
                        v.drain().then_inc(vres, 1)

            @blkc.sync
            def _(sy):
                sy.wait_ge(bar2, 32)
                for mt in range(16):
                    if mt >= 1:
                        sy.wait_ge(vres, mt)
                        sy.wait_ge(pwo, mt)
                    sy.dma_start(out=x_t[:],
                                 in_=x_own[mt * 128:(mt + 1) * 128, :]).then_inc(sx, 16)
                    sy.dma_start(out=y_ld[:].rearrange("p (h c) -> p h c", h=8),
                                 in_=y_dram[:].rearrange("p (h l) -> p h l", h=8)[
                                     :, :, mt * 128:(mt + 1) * 128]).then_inc(syl, 16)
                    sy.wait_ge(vres, mt + 1)
                    sy.wait_ge(stt, 16 * mt)
                    sy.dma_start(out=out[mt * 128:(mt + 1) * 128, :],
                                 in_=o_t[:]).then_inc(stt, 16)
                sy.wait_ge(stt, 16 * 16)

            blkc.__exit__(None, None, None)

    return nc



def _host_masks():
    k = np.arange(128)[:, None]
    q = np.arange(512)[None, :]
    cols = [((t * 128 + k) <= q).astype(ml_dtypes.bfloat16) for t in range(4)]
    return np.ascontiguousarray(np.concatenate(cols, axis=1))


def kernel(L0_attn_x, L0_attn_wq, L0_attn_wk_down, L0_attn_wv_down,
           L0_attn_wk_up, L0_attn_wv_up, L0_attn_wo, L0_ffn_W_fc, L0_ffn_W_proj):
    x = np.ascontiguousarray(np.asarray(L0_attn_x, dtype=np.float32).reshape(T, D))
    wq_h = np.ascontiguousarray(np.asarray(L0_attn_wq, dtype=np.float32))
    wkv_h = np.ascontiguousarray(np.concatenate(
        [np.asarray(L0_attn_wk_down, dtype=np.float32),
         np.asarray(L0_attn_wv_down, dtype=np.float32)], axis=1))
    wk_up_h = np.ascontiguousarray(np.asarray(L0_attn_wk_up, dtype=np.float32))
    wv_up_h = np.ascontiguousarray(np.asarray(L0_attn_wv_up, dtype=np.float32))
    wo_h = np.ascontiguousarray(np.asarray(L0_attn_wo, dtype=np.float32))
    w_fc_h = np.ascontiguousarray(np.asarray(L0_ffn_W_fc, dtype=np.float32))
    w_proj_h = np.ascontiguousarray(np.asarray(L0_ffn_W_proj, dtype=np.float32))
    masks_h = _host_masks()
    ident_h = np.eye(128, dtype=np.float32)

    nc = build_program()
    in_maps = []
    for c in range(NCORES):
        x_own_h = np.ascontiguousarray(np.concatenate(
            [x[c * 1024:(c + 1) * 1024], x[(15 - c) * 1024:(16 - c) * 1024]], axis=0))
        in_maps.append(dict(x_full=x, x_own=x_own_h, wq=wq_h, wkv=wkv_h,
                            wk_up=wk_up_h, wv_up=wv_up_h, wo=wo_h, w_fc=w_fc_h,
                            w_proj=w_proj_h, masks=masks_h, ident=ident_h))
    res = run_bass_kernel_spmd(nc, in_maps, core_ids=list(range(NCORES)))
    out_full = np.empty((T, D), dtype=np.float32)
    for c in range(NCORES):
        o = res.results[c]["out"]
        out_full[c * 1024:(c + 1) * 1024] = o[0:1024]
        out_full[(15 - c) * 1024:(16 - c) * 1024] = o[1024:2048]
    return out_full.reshape(1, T, D)


# version 09
# speedup vs baseline: 1.0028x; 1.0028x over previous
import contextlib

import numpy as np
import ml_dtypes

import concourse.bass as bass
import concourse.mybir as mybir
from concourse.bass_utils import run_bass_kernel_spmd
from concourse.ordered_engine_set import OrderedEngineSet

F32 = mybir.dt.float32
BF16 = mybir.dt.bfloat16
ET = mybir.EngineType
AF = mybir.ActivationFunctionType
ALU = mybir.AluOpType
AX = mybir.AxisListType

NCORES = 8
T = 16384
D = 1024
NH = 8
DLAT = 32
HD = 128
FF = 4096
TOWN = 2048
QSCALE = float(HD) ** -0.5
NGRP = T // 512
VB = 66
VG, AG, PG = 5, 3, 3


def build_program():
    nc = bass.Bass("TRN2", target_bir_lowering=False,
                   detect_race_conditions=False)

    x_full = nc.dram_tensor("x_full", [T, D], F32, kind="ExternalInput").ap()
    x_own = nc.dram_tensor("x_own", [TOWN, D], F32, kind="ExternalInput").ap()
    wq = nc.dram_tensor("wq", [D, D], F32, kind="ExternalInput").ap()
    wkv = nc.dram_tensor("wkv", [D, 128], F32, kind="ExternalInput").ap()
    wk_up = nc.dram_tensor("wk_up", [DLAT, HD], F32, kind="ExternalInput").ap()
    wv_up = nc.dram_tensor("wv_up", [DLAT, HD], F32, kind="ExternalInput").ap()
    wo = nc.dram_tensor("wo", [D, D], F32, kind="ExternalInput").ap()
    w_fc = nc.dram_tensor("w_fc", [FF, D], F32, kind="ExternalInput").ap()
    w_proj = nc.dram_tensor("w_proj", [D, FF], F32, kind="ExternalInput").ap()
    masks_in = nc.dram_tensor("masks", [128, 4 * 512], BF16, kind="ExternalInput").ap()
    ident_in = nc.dram_tensor("ident", [128, 128], F32, kind="ExternalInput").ap()
    out = nc.dram_tensor("out", [TOWN, D], F32, kind="ExternalOutput").ap()

    k_dram = nc.dram_tensor("k_dram", [2 * DLAT, T], BF16, kind="Internal").ap()
    y_dram = nc.dram_tensor("y_dram", [128, NH * TOWN], BF16, kind="Internal").ap()
    v_dram = nc.dram_tensor("v_dram", [128, (T // 128) * VB], BF16, kind="Internal").ap()

    with contextlib.ExitStack() as st:
        E = st.enter_context

        xnT_own = E(nc.sbuf_tensor("xnT_own", [128, 8 * TOWN], BF16))
        qpT = E(nc.sbuf_tensor("qpT", [128, 4 * TOWN], BF16))
        wkv_bf = E(nc.sbuf_tensor("wkv_bf", [128, 8 * 128], BF16))
        wvup_bf = E(nc.sbuf_tensor("wvup_bf", [DLAT, HD], BF16))
        wqp = E(nc.sbuf_tensor("wqp", [128, NH * 8 * DLAT], BF16))
        ident_f = E(nc.sbuf_tensor("ident_f", [128, 128], F32))
        ident_b = E(nc.sbuf_tensor("ident_b", [128, 128], BF16))
        ones64w = E(nc.sbuf_tensor("ones64w", [64, 128], BF16))
        ones32w = E(nc.sbuf_tensor("ones32w", [DLAT, 128], BF16))
        zro33 = E(nc.sbuf_tensor("zro33", [64, 33], BF16))
        zro512 = E(nc.sbuf_tensor("zro512", [64, 512], BF16))
        negB = E(nc.sbuf_tensor("negB", [128, 1], F32))
        maxk = E(nc.sbuf_tensor("maxk", [128, NGRP], F32))
        maxq = E(nc.sbuf_tensor("maxq", [128, 32], F32))

        def sem(name):
            return E(nc.semaphore(name))

        g0 = sem("g0"); gm = sem("gm"); gq = sem("gq"); p0 = sem("p0"); v0 = sem("v0")
        gxp = [sem("gxa"), sem("gxb")]
        sv = sem("sv"); sa = sem("sa"); sp = sem("sp")
        sok = sem("sok"); sov = sem("sov")
        mski = sem("mski"); mskm = sem("mskm")
        gx2p = [sem("gx2a"), sem("gx2b")]
        sv2 = sem("sv2"); sa2 = sem("sa2"); sp2 = sem("sp2")
        sq2p = sem("sq2p"); sq2a = sem("sq2a")
        b0 = sem("b0"); ms = sem("ms")
        sdon = sem("sdon"); edon = sem("edon"); pcons = sem("pcons")
        mdon = sem("mdon"); fdon = sem("fdon")
        ktv = sem("ktv"); ydma = sem("ydma")
        bar = sem("bar"); bar2 = sem("bar2")
        gwo = sem("gwo"); gwp = [sem("gwa"), sem("gwb")]
        pw = sem("pw"); vw = sem("vw")
        pfc = sem("pfc"); ar = sem("ar"); vsq = sem("vsq")
        ppr = sem("ppr"); pwo = sem("pwo"); svk = sem("svk"); vres = sem("vres")
        sx = sem("sx"); stt = sem("stt"); syl = sem("syl")

        with contextlib.ExitStack() as skv:
            Ekv = skv.enter_context
            k_sb = Ekv(nc.sbuf_tensor("k_sb", [128, T], BF16))
            v_sb = Ekv(nc.sbuf_tensor("v_sb", [128, (T // 128) * VB], BF16))

            with contextlib.ExitStack() as s1:
                E1 = s1.enter_context
                x_bf = [E1(nc.sbuf_tensor(f"x_bf{i}", [128, 4 * D], BF16))
                        for i in range(2)]
                xn_sb = E1(nc.sbuf_tensor("xn_sb", [128, 4 * D], BF16))
                xsq_f = E1(nc.sbuf_tensor("xsq_f", [128, 4 * D], F32))
                stat4 = E1(nc.sbuf_tensor("stat4", [128, 512], F32))
                cfA = E1(nc.sbuf_tensor("cfA", [128, 512], F32))
                sq_in = E1(nc.sbuf_tensor("sq_in", [128, 512], F32))
                xnT_g = E1(nc.sbuf_tensor("xnT_g", [128, 4096], BF16))
                kv_ev = E1(nc.sbuf_tensor("kv_ev", [128, 512], BF16))
                sqk = E1(nc.sbuf_tensor("sqk", [64, 512], BF16))
                sq_q = E1(nc.sbuf_tensor("sq_q", [DLAT, 512], BF16))
                wq_bf = E1(nc.sbuf_tensor("wq_bf", [128, 1024], BF16))
                wqT_b = E1(nc.sbuf_tensor("wqT_b", [128, 128], BF16))
                wkupT = E1(nc.sbuf_tensor("wkupT", [HD, DLAT], BF16))
                wkup_b = E1(nc.sbuf_tensor("wkup_b", [DLAT, HD], BF16))
                bsc = E1(nc.sbuf_tensor("bsc", [128, 8], F32))

                psum_tr = E1(nc.psum_tensor("psum_tr", [128, 4096], BF16))
                psum_kv = E1(nc.psum_tensor("psum_kv", [128, 512], F32))
                psum_nrm = E1(nc.psum_tensor("psum_nrm", [128, 512], F32))
                psum_q = E1(nc.psum_tensor("psum_q", [DLAT, 512], F32))

                blk1 = nc.Block()
                blk1.__enter__()

                @blk1.gpsimd
                def _(g):
                    for k in range(8):
                        g.dma_start(out=wkv_bf[:, k * 128:(k + 1) * 128],
                                    in_=wkv[k * 128:(k + 1) * 128, :]).then_inc(g0, 16)
                    g.dma_start(out=wvup_bf[:], in_=wv_up).then_inc(g0, 16)
                    g.dma_start(out=wkup_b[:], in_=wk_up).then_inc(g0, 16)
                    g.dma_start(out=ident_b[:], in_=ident_in).then_inc(g0, 16)
                    g.memset(ones64w[:], 1.0)
                    g.memset(ones32w[:], 1.0)
                    g.memset(zro33[:], 0.0)
                    g.memset(zro512[:], 0.0)
                    g.memset(stat4[:], 1.0)
                    g.memset(sq_in[:], 1.0)
                    g.memset(v_sb[:], 1.0).then_inc(gm, 1)
                    for k in range(8):
                        if k > 0:
                            g.wait_ge(p0, 2 + 16 * k)
                        g.dma_start(out=wq_bf[:],
                                    in_=wq[k * 128:(k + 1) * 128, :]).then_inc(gq, 16)
                    for gi in range(NGRP):
                        if gi >= 2:
                            g.wait_ge(sv, VG * (gi - 2) + 2)
                            g.wait_ge(sa, AG * (gi - 2) + 1)
                        for n in range(4):
                            r0 = 512 * gi + n * 128
                            g.dma_start(out=x_bf[gi % 2][:, n * D:(n + 1) * D],
                                        in_=x_full[r0:r0 + 128, :]
                                        ).then_inc(gxp[gi % 2], 16)

                @blk1.tensor
                def _(t):
                    t.wait_ge(g0, 176)
                    t.wait_ge(gm, 1)
                    t.transpose(psum_tr[:, 0:DLAT], wkup_b[:],
                                ident_b[0:DLAT, 0:DLAT]).then_inc(p0, 1)
                    t.wait_ge(v0, 1)
                    t.nop().then_inc(p0, 1)
                    for k in range(8):
                        t.wait_ge(gq, 16 * (k + 1))
                        for h in range(NH):
                            e = 2 + 16 * k + 2 * h
                            t.wait_ge(v0, 2 * (8 * k + h))
                            t.transpose(psum_tr[:, 0:128], wq_bf[:, h * 128:(h + 1) * 128],
                                        ident_b[:]).then_inc(p0, 1)
                            t.wait_ge(v0, 2 * (8 * k + h) + 2)
                            t.matmul(psum_kv[:, 0:DLAT], wqT_b[:], wkupT[:],
                                     start=True, stop=True).then_inc(p0, 1)
                    t.wait_ge(v0, 129)
                    for gi in range(NGRP):
                        t.wait_ge(sv, VG * gi + 2)
                        if gi > 0:
                            t.wait_ge(sv, VG * (gi - 1) + 5)
                        mm = None
                        for i in range(8):
                            for b in range(4):
                                k = 2 * b + (i // 4)
                                tt = i % 4
                                mm = t.transpose(
                                    psum_tr[:, k * 512 + tt * 128:k * 512 + tt * 128 + 128],
                                    xn_sb[:, tt * D + k * 128:tt * D + k * 128 + 128],
                                    ident_b[:])
                        mm.then_inc(sp, 1)
                        t.wait_ge(sv, VG * gi + 3)
                        if gi > 0:
                            t.wait_ge(sv, VG * (gi - 1) + 4)
                        for k in range(8):
                            mm = t.matmul(psum_kv[:], wkv_bf[:, k * 128:(k + 1) * 128],
                                          xnT_g[:, k * 512:(k + 1) * 512],
                                          start=(k == 0), stop=(k == 7))
                        mm.then_inc(sp, 1)
                        t.wait_ge(sv, VG * gi + 4)
                        t.wait_ge(sa, AG * gi + 3)
                        if gi > 0:
                            t.wait_ge(sv, VG * (gi - 1) + 5)
                        for tt in range(4):
                            t.transpose(psum_tr[:, tt * 1024:tt * 1024 + 64],
                                        kv_ev[64:128, tt * 128:(tt + 1) * 128],
                                        ident_b[64:128, 64:128])
                        t.matmul(psum_nrm[:], ones64w[:], sqk[:], start=True,
                                 stop=True).then_inc(sp, 1)

                @blk1.vector
                def _(v):
                    v.wait_ge(p0, 1)
                    v.tensor_copy(out=wkupT[:], in_=psum_tr[:, 0:DLAT])
                    v.drain().then_inc(v0, 1)
                    for k in range(8):
                        for h in range(NH):
                            e = 2 + 16 * k + 2 * h
                            v.wait_ge(p0, e + 1)
                            v.tensor_copy(out=wqT_b[:], in_=psum_tr[:, 0:128])
                            v.drain().then_inc(v0, 1)
                            v.wait_ge(p0, e + 2)
                            v.tensor_scalar(
                                out=wqp[:, (h * 8 + k) * DLAT:(h * 8 + k + 1) * DLAT],
                                in0=psum_kv[:, 0:DLAT], scalar1=QSCALE, scalar2=None,
                                op0=ALU.mult)
                            v.drain().then_inc(v0, 1)
                    def ln_stats_a(v, xb):
                        for tt in range(4):
                            for h2 in range(2):
                                c0 = tt * D + h2 * 512
                                v.reduce_sum(out=stat4[:, 2 * tt + h2:2 * tt + h2 + 1],
                                             in_=xb[:, c0:c0 + 512], axis=AX.X)

                    def ln_stats_b(v):
                        for tt in range(4):
                            for h2 in range(2):
                                c0 = tt * D + h2 * 512
                                v.reduce_sum(out=stat4[:, 8 + 2 * tt + h2:9 + 2 * tt + h2],
                                             in_=xsq_f[:, c0:c0 + 512], axis=AX.X)
                        v.drain()
                        for tt in range(4):
                            v.tensor_tensor(out=stat4[:, 16 + tt:17 + tt],
                                            in0=stat4[:, 2 * tt:2 * tt + 1],
                                            in1=stat4[:, 2 * tt + 1:2 * tt + 2],
                                            op=ALU.add)
                            v.tensor_tensor(out=stat4[:, 20 + tt:21 + tt],
                                            in0=stat4[:, 8 + 2 * tt:9 + 2 * tt],
                                            in1=stat4[:, 9 + 2 * tt:10 + 2 * tt],
                                            op=ALU.add)
                        v.drain()
                        v.tensor_scalar(out=stat4[:, 12:16], in0=stat4[:, 16:20],
                                        scalar1=1.0 / 1024.0, scalar2=None,
                                        op0=ALU.mult)
                        v.tensor_scalar(out=stat4[:, 24:28], in0=stat4[:, 20:24],
                                        scalar1=1.0 / 1024.0, scalar2=None,
                                        op0=ALU.mult)
                        v.drain()
                        v.tensor_tensor(out=stat4[:, 28:32], in0=stat4[:, 12:16],
                                        in1=stat4[:, 12:16], op=ALU.mult)
                        v.drain()
                        v.tensor_tensor(out=stat4[:, 28:32], in0=stat4[:, 24:28],
                                        in1=stat4[:, 28:32], op=ALU.subtract)
                        v.drain()
                        v.tensor_scalar(out=sq_in[:, 24:28], in0=stat4[:, 28:32],
                                        scalar1=1e-5, scalar2=None, op0=ALU.add)
                        return v.drain()

                    for gi in range(NGRP):
                        v.wait_ge(gxp[gi % 2], 64 * (gi // 2 + 1))
                        if gi > 0:
                            v.wait_ge(sa, AG * (gi - 1) + 2)
                        ln_stats_a(v, x_bf[gi % 2])
                        v.wait_ge(sa, AG * gi + 1)
                        ln_stats_b(v).then_inc(sv, 1)
                        v.wait_ge(sa, AG * gi + 2)
                        v.reciprocal(out=stat4[:, 0:4], in_=cfA[:, 24:28])
                        v.drain()
                        v.scalar_tensor_tensor(out=stat4[:, 16:20],
                                               in0=stat4[:, 12:16], scalar=-1.0,
                                               in1=stat4[:, 0:4],
                                               op0=ALU.mult, op1=ALU.mult)
                        v.drain()
                        if gi > 0:
                            v.wait_ge(sp, PG * (gi - 1) + 1)
                        for tt in range(4):
                            v.tensor_scalar(out=xn_sb[:, tt * D:(tt + 1) * D],
                                            in0=x_bf[gi % 2][:, tt * D:(tt + 1) * D],
                                            scalar1=stat4[:, tt:tt + 1],
                                            scalar2=stat4[:, 16 + tt:17 + tt],
                                            op0=ALU.mult, op1=ALU.add)
                        v.drain().then_inc(sv, 1)
                        v.wait_ge(sp, PG * gi + 1)
                        v.tensor_copy(out=xnT_g[:], in_=psum_tr[:])
                        v.drain().then_inc(sv, 1)
                        v.wait_ge(sp, PG * gi + 2)
                        if gi > 0:
                            v.wait_ge(sok, 16 * gi)
                            v.wait_ge(sa, AG * (gi - 1) + 3)
                            v.wait_ge(sp, PG * (gi - 1) + 3)
                        v.tensor_copy(out=kv_ev[:], in_=psum_kv[:])
                        v.tensor_copy(out=k_sb[0:32, 512 * gi:512 * (gi + 1)],
                                      in_=psum_kv[0:32, :])
                        v.tensor_copy(out=k_sb[64:96, 512 * gi:512 * (gi + 1)],
                                      in_=psum_kv[32:64, :])
                        v.drain().then_inc(sv, 1)
                        v.wait_ge(sp, PG * gi + 3)
                        for tt in range(4):
                            vb0 = 4 * VB * gi + tt * VB
                            v.tensor_copy(out=v_sb[:, vb0:vb0 + 32],
                                          in_=psum_tr[:, tt * 1024:tt * 1024 + 32])
                            v.tensor_copy(out=v_sb[:, vb0 + 33:vb0 + 65],
                                          in_=psum_tr[:, tt * 1024 + 32:tt * 1024 + 64])
                        v.drain()
                        v.reduce_max(out=maxk[:, gi:gi + 1], in_=psum_nrm[:],
                                     axis=AX.X)
                        v.drain().then_inc(sv, 1)

                @blk1.scalar
                def _(sc):
                    for gi in range(NGRP):
                        sc.wait_ge(gxp[gi % 2], 64 * (gi // 2 + 1))
                        if gi > 0:
                            sc.wait_ge(sv, VG * (gi - 1) + 1)
                        for q4 in range(4):
                            sc.activation(out=xsq_f[:, q4 * D:(q4 + 1) * D],
                                          in_=x_bf[gi % 2][:, q4 * D:(q4 + 1) * D],
                                          func=AF.Square)
                        sc.drain().then_inc(sa, 1)
                        sc.wait_ge(sv, VG * gi + 1)
                        sc.activation(out=cfA[:], in_=sq_in[:], func=AF.Sqrt,
                                      bias=0.0, scale=1.0)
                        sc.drain().then_inc(sa, 1)
                        sc.wait_ge(sv, VG * gi + 4)
                        if gi > 0:
                            sc.wait_ge(sp, PG * (gi - 1) + 3)
                        sc.activation(out=sqk[:], in_=kv_ev[0:64, :],
                                      func=AF.Square)
                        sc.drain().then_inc(sa, 1)

                @blk1.sync
                def _(sy):
                    sy.dma_start(out=ident_f[:], in_=ident_in).then_inc(mski, 16)
                    for gi in range(NGRP):
                        sy.wait_ge(sv, VG * gi + 4)
                        sy.dma_start(out=k_dram[:, 512 * gi:512 * (gi + 1)],
                                     in_=kv_ev[0:64, :]).then_inc(sok, 16)
                        sy.wait_ge(sv, VG * gi + 5)
                        sy.dma_start(out=v_dram[:, 4 * VB * gi:4 * VB * (gi + 1)],
                                     in_=v_sb[:, 4 * VB * gi:4 * VB * (gi + 1)]
                                     ).then_inc(sov, 16)

                blk1.__exit__(None, None, None)

                blk2 = nc.Block()
                blk2.__enter__()

                @blk2.gpsimd
                def _(g):
                    for gi in range(4):
                        if gi < 2:
                            g.wait_ge(sv, VG * (NGRP - 1) + 2)
                            g.wait_ge(sa, AG * (NGRP - 1) + 1)
                        else:
                            g.wait_ge(sv2, 3 * (gi - 2) + 2)
                            g.wait_ge(sa2, 2 * (gi - 2) + 1)
                        for n in range(4):
                            r0 = 512 * gi + n * 128
                            g.dma_start(out=x_bf[gi % 2][:, n * D:(n + 1) * D],
                                        in_=x_own[r0:r0 + 128, :]
                                        ).then_inc(gx2p[gi % 2], 16)

                @blk2.vector
                def _(v):
                    for gi in range(4):
                        v.wait_ge(gx2p[gi % 2], 64 * (gi // 2 + 1))
                        if gi == 0:
                            v.wait_ge(sa, AG * (NGRP - 1) + 2)
                        else:
                            v.wait_ge(sa2, 2 * (gi - 1) + 2)
                        xb2 = x_bf[gi % 2]
                        for tt in range(4):
                            for h2 in range(2):
                                c0 = tt * D + h2 * 512
                                v.reduce_sum(out=stat4[:, 2 * tt + h2:2 * tt + h2 + 1],
                                             in_=xb2[:, c0:c0 + 512], axis=AX.X)
                        v.wait_ge(sa2, 2 * gi + 1)
                        for tt in range(4):
                            for h2 in range(2):
                                c0 = tt * D + h2 * 512
                                v.reduce_sum(out=stat4[:, 8 + 2 * tt + h2:9 + 2 * tt + h2],
                                             in_=xsq_f[:, c0:c0 + 512], axis=AX.X)
                        v.drain()
                        for tt in range(4):
                            v.tensor_tensor(out=stat4[:, 16 + tt:17 + tt],
                                            in0=stat4[:, 2 * tt:2 * tt + 1],
                                            in1=stat4[:, 2 * tt + 1:2 * tt + 2],
                                            op=ALU.add)
                            v.tensor_tensor(out=stat4[:, 20 + tt:21 + tt],
                                            in0=stat4[:, 8 + 2 * tt:9 + 2 * tt],
                                            in1=stat4[:, 9 + 2 * tt:10 + 2 * tt],
                                            op=ALU.add)
                        v.drain()
                        v.tensor_scalar(out=stat4[:, 12:16], in0=stat4[:, 16:20],
                                        scalar1=1.0 / 1024.0, scalar2=None,
                                        op0=ALU.mult)
                        v.tensor_scalar(out=stat4[:, 24:28], in0=stat4[:, 20:24],
                                        scalar1=1.0 / 1024.0, scalar2=None,
                                        op0=ALU.mult)
                        v.drain()
                        v.tensor_tensor(out=stat4[:, 28:32], in0=stat4[:, 12:16],
                                        in1=stat4[:, 12:16], op=ALU.mult)
                        v.drain()
                        v.tensor_tensor(out=stat4[:, 28:32], in0=stat4[:, 24:28],
                                        in1=stat4[:, 28:32], op=ALU.subtract)
                        v.drain()
                        v.tensor_scalar(out=sq_in[:, 24:28], in0=stat4[:, 28:32],
                                        scalar1=1e-5, scalar2=None, op0=ALU.add)
                        v.drain().then_inc(sv2, 1)
                        v.wait_ge(sa2, 2 * gi + 2)
                        v.reciprocal(out=stat4[:, 0:4], in_=cfA[:, 24:28])
                        v.drain()
                        v.scalar_tensor_tensor(out=stat4[:, 16:20],
                                               in0=stat4[:, 12:16], scalar=-1.0,
                                               in1=stat4[:, 0:4],
                                               op0=ALU.mult, op1=ALU.mult)
                        v.drain()
                        if gi == 0:
                            v.wait_ge(sp, PG * (NGRP - 1) + 1)
                        else:
                            v.wait_ge(sp2, gi)
                        for tt in range(4):
                            v.tensor_scalar(out=xn_sb[:, tt * D:(tt + 1) * D],
                                            in0=x_bf[gi % 2][:, tt * D:(tt + 1) * D],
                                            scalar1=stat4[:, tt:tt + 1],
                                            scalar2=stat4[:, 16 + tt:17 + tt],
                                            op0=ALU.mult, op1=ALU.add)
                        v.drain().then_inc(sv2, 1)
                        v.wait_ge(sp2, gi + 1)
                        for k in range(8):
                            v.tensor_copy(
                                out=xnT_own[:, k * TOWN + gi * 512:k * TOWN + (gi + 1) * 512],
                                in_=psum_tr[:, k * 512:(k + 1) * 512])
                        v.drain().then_inc(sv2, 1)

                @blk2.scalar
                def _(sc):
                    for gi in range(4):
                        sc.wait_ge(gx2p[gi % 2], 64 * (gi // 2 + 1))
                        if gi > 0:
                            sc.wait_ge(sv2, 3 * (gi - 1) + 1)
                        for q4 in range(4):
                            sc.activation(out=xsq_f[:, q4 * D:(q4 + 1) * D],
                                          in_=x_bf[gi % 2][:, q4 * D:(q4 + 1) * D],
                                          func=AF.Square)
                        sc.drain().then_inc(sa2, 1)
                        sc.wait_ge(sv2, 3 * gi + 1)
                        sc.activation(out=cfA[:], in_=sq_in[:], func=AF.Sqrt,
                                      bias=0.0, scale=1.0)
                        sc.drain().then_inc(sa2, 1)
                        for h in range(NH):
                            sc.wait_ge(sq2p, 8 * gi + h + 1)
                            qb = 64 * (h // 4)
                            qc0 = (h % 4) * TOWN + gi * 512
                            sc.copy(out=qpT[qb:qb + DLAT, qc0:qc0 + 512],
                                    in_=psum_q[:])
                            sc.drain().then_inc(sq2a, 1)

                @blk2.tensor
                def _(t):
                    for gi in range(4):
                        t.wait_ge(sv2, 3 * gi + 2)
                        if gi == 0:
                            t.wait_ge(sv, VG * (NGRP - 1) + 5)
                        else:
                            t.wait_ge(sv2, 3 * (gi - 1) + 3)
                        mm = None
                        for i in range(8):
                            for b in range(4):
                                k = 2 * b + (i // 4)
                                tt = i % 4
                                mm = t.transpose(
                                    psum_tr[:, k * 512 + tt * 128:k * 512 + tt * 128 + 128],
                                    xn_sb[:, tt * D + k * 128:tt * D + k * 128 + 128],
                                    ident_b[:])
                        mm.then_inc(sp2, 1)
                        t.wait_ge(sv2, 3 * gi + 3)
                        for h in range(NH):
                            t.wait_ge(sq2a, 8 * gi + h)
                            for k in range(8):
                                mm = t.matmul(
                                    psum_q[:],
                                    wqp[:, (h * 8 + k) * DLAT:(h * 8 + k + 1) * DLAT],
                                    xnT_own[:, k * TOWN + gi * 512:k * TOWN + (gi + 1) * 512],
                                    start=(k == 0), stop=(k == 7))
                            mm.then_inc(sq2p, 1)

                blk2.__exit__(None, None, None)

                blk3 = nc.Block()
                blk3.__enter__()

                @blk3.scalar
                def _(sc):
                    for ch in range(32):
                        h, gi4 = ch // 4, ch % 4
                        qb = 64 * (h // 4)
                        qc0 = (h % 4) * TOWN + gi4 * 512
                        sc.wait_ge(b0, 3 * ch)
                        sc.activation(out=sq_q[:], in_=qpT[qb:qb + DLAT, qc0:qc0 + 512],
                                      func=AF.Square)
                        sc.drain().then_inc(b0, 1)
                    sc.wait_ge(b0, 97)
                    sc.activation(out=cfA[:], in_=sq_in[:], func=AF.Sqrt,
                                  bias=0.0, scale=1.0)
                    sc.drain().then_inc(b0, 1)

                @blk3.tensor
                def _(t):
                    t.wait_ge(sv, VG * NGRP)
                    for ch in range(32):
                        t.wait_ge(b0, 3 * ch + 1)
                        t.matmul(psum_nrm[:], ones32w[:], sq_q[:],
                                 start=True, stop=True).then_inc(b0, 1)

                @blk3.vector
                def _(v):
                    for ch in range(32):
                        v.wait_ge(b0, 3 * ch + 2)
                        v.reduce_max(out=maxq[:, ch:ch + 1], in_=psum_nrm[:],
                                     axis=AX.X)
                        v.drain().then_inc(b0, 1)
                    v.reduce_max(out=bsc[:, 0:1], in_=maxq[:], axis=AX.X)
                    v.reduce_max(out=bsc[:, 1:2], in_=maxk[:], axis=AX.X)
                    v.drain()
                    v.tensor_tensor(out=bsc[:, 2:3], in0=bsc[:, 0:1], in1=bsc[:, 1:2],
                                    op=ALU.mult)
                    v.drain()
                    v.tensor_scalar(out=sq_in[:, 24:25], in0=bsc[:, 2:3],
                                    scalar1=1e-30, scalar2=None, op0=ALU.add)
                    v.drain().then_inc(b0, 1)
                    v.wait_ge(b0, 98)
                    v.tensor_scalar(out=negB[:], in0=cfA[:, 24:25], scalar1=-1.0,
                                    scalar2=None, op0=ALU.mult)
                    v.drain().then_inc(ms, 1)

                blk3.__exit__(None, None, None)

            with contextlib.ExitStack() as s2:
                E2 = s2.enter_context
                ktail = E2(nc.sbuf_tensor("ktail", [128, 2048], BF16))
                vtail = E2(nc.sbuf_tensor("vtail", [128, 2 * 8 * VB], BF16))
                pt1 = E2(nc.sbuf_tensor("pt1", [128, 2048], BF16))
                pt2 = E2(nc.sbuf_tensor("pt2", [128, 2048], BF16))
                pm = E2(nc.sbuf_tensor("pm", [128, 2048], BF16))
                olat_sb = E2(nc.sbuf_tensor("olat_sb", [33, 512], F32))
                onorm = E2(nc.sbuf_tensor("onorm", [128, 128], BF16))
                rec4 = E2(nc.sbuf_tensor("rec4", [128, 4], F32))
                olatTn = E2(nc.sbuf_tensor("olatTn", [DLAT, 512], BF16))
                y_stage = E2(nc.sbuf_tensor("y_stage", [128, 512], BF16))
                masks = E2(nc.sbuf_tensor("masks_sb", [128, 4 * 512], BF16))

                psum_s = E2(nc.psum_tensor("psum_s", [128, 2048], F32))
                psum_o = E2(nc.psum_tensor("psum_o", [33, 512], F32))
                psum_f2 = [E2(nc.psum_tensor(f"psum_f2{i}", [DLAT, 256], BF16))
                           for i in range(2)]
                psum_yb = E2(nc.psum_tensor("psum_yb", [128, 512], F32))

                qtiles = [(h, blk, qt) for h in range(NH) for blk in range(2)
                          for qt in range(2)]
                PB_SEMS = [sdon, edon, mdon, fdon]

                def skipchain(eng, et, rj, ri, maxj, body_fn, tag):
                    engines = OrderedEngineSet([et])
                    for b in range(maxj):
                        lid = nc.next_id()
                        l0, l1 = f"{tag}{lid}_b", f"{tag}{lid}_e"
                        eng.reg_mov(ri, b)
                        eng.br_cmp(ri, rj, on_true=l0, on_false=l1,
                                   comp_op="IS_LT")
                        with nc.body(l0, valid_engines=engines):
                            body_fn(b)
                            eng.reg_alu(ri, ri, 64, op=ALU.add)
                            eng.br_cmp(ri, rj, on_true=l0, on_false=l1,
                                       comp_op="IS_LT")
                        nc.switch_bb(l1)

                def wge(eng, sem_h, base_reg, mul, add, rt):
                    eng.reg_alu(rt, base_reg, mul, op=ALU.mult)
                    if add:
                        eng.reg_alu(rt, rt, add, op=ALU.add)
                    eng.wait_ge(sem_h, rt)

                blkb = nc.Block()
                blkb.__enter__()

                @blkb.sync
                def _(sy):
                    sy.dma_start(out=masks[:], in_=masks_in).then_inc(mskm, 16)
                    pid_sp = nc.partition_id([ET.SP])
                    rB = sy.alloc_register("sy_rB")
                    rt0 = sy.alloc_register("sy_rt0")
                    sy.reg_mov(rt0, 15)
                    sy.reg_alu(rB, rt0, pid_sp, op=ALU.subtract)
                    jB = nc.snap(rB, min_val=8, max_val=15)
                    jA = nc.snap(pid_sp, min_val=0, max_val=7)
                    for blk, jj in ((0, jA), (1, jB)):
                        for g in range(2):
                            sy.dma_start(
                                out=ktail[64 * g:64 * g + DLAT,
                                          blk * 1024:(blk + 1) * 1024],
                                in_=k_dram[32 * g:32 * g + DLAT, bass.ts(jj, 1024)]
                                ).then_inc(ktv, 16)
                        sy.dma_start(out=vtail[:, blk * 8 * VB:(blk + 1) * 8 * VB],
                                     in_=v_dram[:, bass.ts(jj, 8 * VB)]
                                     ).then_inc(ktv, 16)
                    for qi, (h, blk, qt) in enumerate(qtiles):
                        qcol = h * TOWN + blk * 1024 + qt * 512
                        if qi > 0:
                            sy.wait_ge(bar2, qi)
                        sy.wait_ge(fdon, 8)
                        sy.dma_start(out=y_dram[:, qcol:qcol + 512],
                                     in_=y_stage[:]).then_inc(ydma, 16)
                        sy.wait_ge(ydma, 16 * (qi + 1))
                        sy.nop().then_inc(bar, 1)

                @blkb.tensor
                def _(t):
                    pid_pe = nc.partition_id([ET.PE])
                    rjA = t.alloc_register("pe_rjA")
                    rjB = t.alloc_register("pe_rjB")
                    rt = t.alloc_register("pe_rt")
                    ri = t.alloc_register("pe_ri")
                    t.reg_alu(rjA, pid_pe, 0, op=ALU.add)
                    t.reg_mov(rt, 15)
                    t.reg_alu(rjB, rt, pid_pe, op=ALU.subtract)
                    t.wait_ge(ms, 1)
                    t.wait_ge(mski, 16)
                    t.wait_ge(ktv, 96)
                    for qi, (h, blk, qt) in enumerate(qtiles):
                        g = h // 4
                        qb = 64 * g
                        qpcol = (h % 4) * TOWN + blk * 1024 + qt * 512
                        rj = rjA if blk == 0 else rjB
                        maxj = 7 if blk == 0 else 15
                        if qi > 0:
                            t.wait_ge(bar2, qi)
                        t.matmul(psum_o[:], zro33[:], zro512[:], start=True,
                                 stop=False)

                        def pe_body(b):
                            t.wait_ge(edon, 2 * b)
                            mm = None
                            for w in range(4):
                                mm = t.matmul(psum_s[:, w * 512:(w + 1) * 512],
                                              k_sb[qb:qb + DLAT,
                                                   (8 * b + w) * 128:(8 * b + w + 1) * 128],
                                              qpT[qb:qb + DLAT, qpcol:qpcol + 512],
                                              start=True, stop=True)
                            mm.then_inc(sdon, 1)
                            t.wait_ge(edon, 2 * b + 1)
                            for w in range(4, 8):
                                mm = t.matmul(psum_s[:, (w - 4) * 512:(w - 3) * 512],
                                              k_sb[qb:qb + DLAT,
                                                   (8 * b + w) * 128:(8 * b + w + 1) * 128],
                                              qpT[qb:qb + DLAT, qpcol:qpcol + 512],
                                              start=True, stop=True)
                            mm.then_inc(sdon, 1)
                            for w in range(4):
                                mm = t.matmul(
                                    psum_o[:],
                                    v_sb[:, (8 * b + w) * VB + 33 * g:
                                         (8 * b + w) * VB + 33 * g + 33],
                                    pt1[:, w * 512:(w + 1) * 512],
                                    start=False, stop=False)
                            t.wait_ge(edon, 2 * b + 2)
                            for w in range(4, 8):
                                mm = t.matmul(
                                    psum_o[:],
                                    v_sb[:, (8 * b + w) * VB + 33 * g:
                                         (8 * b + w) * VB + 33 * g + 33],
                                    pt2[:, (w - 4) * 512:(w - 3) * 512],
                                    start=False, stop=False)

                        skipchain(t, ET.PE, rj, ri, maxj, pe_body, "pe")

                        kt0 = blk * 1024
                        vt0 = blk * 8 * VB
                        wge(t, edon, rj, 2, 0, rt)
                        if qt == 0:
                            mm = None
                            for w in range(4):
                                mm = t.matmul(psum_s[:, w * 512:(w + 1) * 512],
                                              ktail[qb:qb + DLAT,
                                                    kt0 + w * 128:kt0 + (w + 1) * 128],
                                              qpT[qb:qb + DLAT, qpcol:qpcol + 512],
                                              start=True, stop=True)
                            mm.then_inc(sdon, 1)
                            t.wait_ge(mdon, 1)
                            for w in range(4):
                                mm = t.matmul(
                                    psum_o[:],
                                    vtail[:, vt0 + w * VB + 33 * g:
                                          vt0 + w * VB + 33 * g + 33],
                                    pm[:, w * 512:(w + 1) * 512],
                                    start=False, stop=(w == 3))
                            mm.then_inc(fdon, 1)
                        else:
                            mm = None
                            for w in range(4):
                                mm = t.matmul(psum_s[:, w * 512:(w + 1) * 512],
                                              ktail[qb:qb + DLAT,
                                                    kt0 + w * 128:kt0 + (w + 1) * 128],
                                              qpT[qb:qb + DLAT, qpcol:qpcol + 512],
                                              start=True, stop=True)
                            mm.then_inc(sdon, 1)
                            wge(t, edon, rj, 2, 1, rt)
                            for w in range(4, 8):
                                mm = t.matmul(psum_s[:, (w - 4) * 512:(w - 3) * 512],
                                              ktail[qb:qb + DLAT,
                                                    kt0 + w * 128:kt0 + (w + 1) * 128],
                                              qpT[qb:qb + DLAT, qpcol:qpcol + 512],
                                              start=True, stop=True)
                            mm.then_inc(sdon, 1)
                            for w in range(4):
                                t.matmul(psum_o[:],
                                         vtail[:, vt0 + w * VB + 33 * g:
                                               vt0 + w * VB + 33 * g + 33],
                                         pt1[:, w * 512:(w + 1) * 512],
                                         start=False, stop=False)
                            t.wait_ge(mdon, 1)
                            for w in range(4, 8):
                                mm = t.matmul(
                                    psum_o[:],
                                    vtail[:, vt0 + w * VB + 33 * g:
                                          vt0 + w * VB + 33 * g + 33],
                                    pm[:, (w - 4) * 512:(w - 3) * 512],
                                    start=False, stop=(w == 7))
                            mm.then_inc(fdon, 1)

                        t.wait_ge(fdon, 2)
                        for tt in range(4):
                            mm = t.transpose(psum_s[:, tt * 512:tt * 512 + 33],
                                             olat_sb[:, tt * 128:(tt + 1) * 128],
                                             ident_f[0:33, 0:33])
                        mm.then_inc(fdon, 1)
                        t.wait_ge(fdon, 4)
                        for tt in range(4):
                            mm = t.transpose(
                                psum_f2[tt % 2][:, (tt // 2) * 128:(tt // 2) * 128 + 128],
                                onorm[:, tt * 32:(tt + 1) * 32], ident_b[:])
                        mm.then_inc(fdon, 1)
                        t.wait_ge(fdon, 6)
                        t.matmul(psum_yb[:], wvup_bf[:], olatTn[:], start=True,
                                 stop=True).then_inc(fdon, 1)
                        t.nop().then_inc(bar, 1)

                @blkb.scalar
                def _(sc):
                    pid_ac = nc.partition_id([ET.Activation])
                    rjA = sc.alloc_register("ac_rjA")
                    rjB = sc.alloc_register("ac_rjB")
                    rt = sc.alloc_register("ac_rt")
                    ri = sc.alloc_register("ac_ri")
                    sc.reg_alu(rjA, pid_ac, 0, op=ALU.add)
                    sc.reg_mov(rt, 15)
                    sc.reg_alu(rjB, rt, pid_ac, op=ALU.subtract)
                    sc.wait_ge(ms, 1)
                    for qi, (h, blk, qt) in enumerate(qtiles):
                        rj = rjA if blk == 0 else rjB
                        maxj = 7 if blk == 0 else 15
                        if qi > 0:
                            sc.wait_ge(bar2, qi)

                        def act_body(b):
                            sc.wait_ge(sdon, 2 * b + 1)
                            sc.activation(out=pt1[:], in_=psum_s[:], func=AF.Exp,
                                          bias=negB[:, 0:1], scale=1.0)
                            sc.drain().then_inc(edon, 1)
                            sc.wait_ge(sdon, 2 * b + 2)
                            sc.activation(out=pt2[:], in_=psum_s[:], func=AF.Exp,
                                          bias=negB[:, 0:1], scale=1.0)
                            sc.drain().then_inc(edon, 1)

                        skipchain(sc, ET.Activation, rj, ri, maxj, act_body, "ac")

                        wge(sc, sdon, rj, 2, 1, rt)
                        sc.activation(out=pt1[:], in_=psum_s[:], func=AF.Exp,
                                      bias=negB[:, 0:1], scale=1.0)
                        sc.drain().then_inc(edon, 1)
                        if qt == 1:
                            wge(sc, sdon, rj, 2, 2, rt)
                            sc.activation(out=pt2[:], in_=psum_s[:], func=AF.Exp,
                                          bias=negB[:, 0:1], scale=1.0)
                            sc.drain().then_inc(edon, 1)
                        sc.wait_ge(fdon, 7)
                        sc.copy(out=y_stage[:], in_=psum_yb[:])
                        sc.drain().then_inc(fdon, 1)
                        sc.nop().then_inc(bar, 1)

                @blkb.vector
                def _(v):
                    pid_dv = nc.partition_id([ET.DVE])
                    rjA = v.alloc_register("dv_rjA")
                    rjB = v.alloc_register("dv_rjB")
                    rt = v.alloc_register("dv_rt")
                    v.reg_alu(rjA, pid_dv, 0, op=ALU.add)
                    v.reg_mov(rt, 15)
                    v.reg_alu(rjB, rt, pid_dv, op=ALU.subtract)
                    v.wait_ge(ms, 1)
                    v.wait_ge(mskm, 16)
                    for qi, (h, blk, qt) in enumerate(qtiles):
                        rj = rjA if blk == 0 else rjB
                        if qi > 0:
                            v.wait_ge(bar2, qi)
                        srcb = pt1 if qt == 0 else pt2
                        wge(v, edon, rj, 2, 1 if qt == 0 else 2, rt)
                        for w in range(4):
                            v.tensor_tensor(out=pm[:, w * 512:(w + 1) * 512],
                                            in0=srcb[:, w * 512:(w + 1) * 512],
                                            in1=masks[:, w * 512:(w + 1) * 512],
                                            op=ALU.mult)
                        v.drain().then_inc(mdon, 1)
                        v.wait_ge(fdon, 1)
                        v.tensor_copy(out=olat_sb[:], in_=psum_o[:])
                        v.drain().then_inc(fdon, 1)
                        v.wait_ge(fdon, 3)
                        for tt in range(4):
                            v.reciprocal(out=rec4[:, tt:tt + 1],
                                         in_=psum_s[:, tt * 512 + 32:tt * 512 + 33])
                        v.drain()
                        for tt in range(4):
                            v.tensor_scalar(out=onorm[:, tt * 32:(tt + 1) * 32],
                                            in0=psum_s[:, tt * 512:tt * 512 + 32],
                                            scalar1=rec4[:, tt:tt + 1],
                                            scalar2=None, op0=ALU.mult)
                        v.drain().then_inc(fdon, 1)
                        v.wait_ge(fdon, 5)
                        for tt in range(4):
                            v.tensor_copy(
                                out=olatTn[:, tt * 128:(tt + 1) * 128],
                                in_=psum_f2[tt % 2][:, (tt // 2) * 128:(tt // 2) * 128 + 128])
                        v.drain().then_inc(fdon, 1)
                        v.nop().then_inc(bar, 1)
                        v.wait_ge(bar, 4 * (qi + 1))
                        for s_ in PB_SEMS:
                            v.sem_clear(s_)
                        v.nop().then_inc(bar2, 1)

                blkb.__exit__(None, None, None)

        with contextlib.ExitStack() as s3:
            E3 = s3.enter_context
            wpT = E3(nc.sbuf_tensor("wpT", [128, 32 * 1024], BF16))
            hT = E3(nc.sbuf_tensor("hT", [128, 32 * 512], BF16))
            wo_sb = E3(nc.sbuf_tensor("wo_sb", [128, 8 * 1024], BF16))
            wraw = [E3(nc.sbuf_tensor(f"wraw{i}", [128, 1024], BF16)) for i in range(2)]
            wfT = [E3(nc.sbuf_tensor(f"wfT{i}", [128, 1024], BF16)) for i in range(2)]
            r_bf = [E3(nc.sbuf_tensor(f"r_bf{i}", [128, 512], BF16)) for i in range(2)]
            yf_sb = E3(nc.sbuf_tensor("yf_sb", [128, 1024], F32))
            mem_f = E3(nc.sbuf_tensor("mem_f", [128, 1024], F32))
            spk_f = E3(nc.sbuf_tensor("spk_f", [128, 1024], F32))
            acc_f = E3(nc.sbuf_tensor("acc_f", [128, 1024], F32))
            x_t = E3(nc.sbuf_tensor("x_t", [128, 1024], F32))
            o_t = E3(nc.sbuf_tensor("o_t", [128, 1024], F32))
            y_ld = E3(nc.sbuf_tensor("y_ld", [128, 8 * 128], BF16))

            psum_w = E3(nc.psum_tensor("psum_w", [128, 2048], BF16))
            psum_fc = [E3(nc.psum_tensor(f"psum_fc{i}", [128, 512], F32))
                       for i in range(2)]
            psum_yf = [E3(nc.psum_tensor(f"psum_yf{i}", [128, 1024], F32))
                       for i in range(2)]

            blkc = nc.Block()
            blkc.__enter__()

            @blkc.gpsimd
            def _(g):
                g.wait_ge(bar2, 32)
                for k in range(8):
                    g.dma_start(out=wo_sb[:, k * 1024:(k + 1) * 1024],
                                in_=wo[k * 128:(k + 1) * 128, :]).then_inc(gwo, 16)
                for ph in range(32):
                    if ph >= 2:
                        g.wait_ge(pw, ph - 1)
                    for k in range(8):
                        g.dma_start(out=wraw[ph % 2][:, k * 128:(k + 1) * 128],
                                    in_=w_proj[k * 128:(k + 1) * 128,
                                               ph * 128:(ph + 1) * 128]
                                    ).then_inc(gwp[ph % 2], 16)
                for nn in range(128):
                    ph = nn % 32
                    g.wait_ge(pw, 31 + nn)
                    g.dma_start(out=wraw[nn % 2][:],
                                in_=w_fc[ph * 128:(ph + 1) * 128, :]).then_inc(gwp[nn % 2], 16)

            @blkc.tensor
            def _(t):
                t.wait_ge(bar2, 32)
                for ph in range(32):
                    t.wait_ge(gwp[ph % 2], 128 * (ph // 2 + 1))
                    if ph > 0:
                        t.wait_ge(vw, ph)
                    mm = None
                    for k in range(8):
                        pc = (k % 2) * 1024 + (k // 2) * 128
                        mm = t.transpose(psum_w[:, pc:pc + 128],
                                         wraw[ph % 2][:, k * 128:(k + 1) * 128],
                                         ident_b[:])
                    mm.then_inc(pw, 1)
                for hb in range(4):
                    for ph in range(32):
                        nn = hb * 32 + ph
                        t.wait_ge(gwp[nn % 2], 128 * 16 + 16 * (nn // 2 + 1))
                        t.wait_ge(vw, 32 + nn)
                        mm = None
                        for k in range(8):
                            pc = (k % 2) * 1024 + (k // 2) * 128
                            mm = t.transpose(psum_w[:, pc:pc + 128],
                                             wraw[nn % 2][:, k * 128:(k + 1) * 128],
                                             ident_b[:])
                        mm.then_inc(pw, 1)
                        t.wait_ge(vw, 32 + nn + 1)
                        if nn >= 2:
                            t.wait_ge(ar, nn - 1)
                        for k in range(8):
                            mm = t.matmul(
                                psum_fc[ph % 2][:],
                                wfT[nn % 2][:, k * 128:(k + 1) * 128],
                                xnT_own[:, k * TOWN + hb * 512:k * TOWN + hb * 512 + 512],
                                start=(k == 0), stop=(k == 7))
                        mm.then_inc(pfc, 1)
                    for m in range(4):
                        mt = hb * 4 + m
                        t.wait_ge(vsq, 32 * (hb + 1))
                        if mt >= 2:
                            t.wait_ge(vres, mt - 1)
                        for ph in range(32):
                            for nf in range(2):
                                mm = t.matmul(
                                    psum_yf[m % 2][:, nf * 512:(nf + 1) * 512],
                                    hT[:, ph * 512 + m * 128:ph * 512 + m * 128 + 128],
                                    wpT[:, ph * 1024 + nf * 512:ph * 1024 + (nf + 1) * 512],
                                    start=(ph == 0), stop=(ph == 31))
                        mm.then_inc(ppr, 1)
                        t.wait_ge(svk, mt + 1)
                        t.wait_ge(syl, 16 * (mt + 1))
                        t.wait_ge(gwo, 128)
                        for nf in range(2):
                            for hh in range(NH):
                                mm = t.matmul(
                                    psum_yf[m % 2][:, nf * 512:(nf + 1) * 512],
                                    y_ld[:, hh * 128:(hh + 1) * 128],
                                    wo_sb[:, hh * 1024 + nf * 512:hh * 1024 + (nf + 1) * 512],
                                    start=(hh == 0), stop=(hh == 7))
                        mm.then_inc(pwo, 1)

            @blkc.scalar
            def _(sc):
                sc.wait_ge(bar2, 32)
                for nn in range(128):
                    ph = nn % 32
                    sc.wait_ge(pfc, nn + 1)
                    if nn >= 2:
                        sc.wait_ge(vsq, nn - 1)
                    sc.activation(out=r_bf[ph % 2][:], in_=psum_fc[ph % 2][:],
                                  func=AF.Relu)
                    sc.drain().then_inc(ar, 1)

            @blkc.vector
            def _(v):
                v.wait_ge(bar2, 32)
                for ph in range(32):
                    v.wait_ge(pw, ph + 1)
                    v.tensor_copy(
                        out=wpT[:, ph * 1024:(ph + 1) * 1024].rearrange(
                            "p (r b c) -> p r b c", r=4, b=2),
                        in_=psum_w[:].rearrange("p (b r c) -> p r b c",
                                                b=2, r=8)[:, 0:4, :, :])
                    v.drain().then_inc(vw, 1)
                for hb in range(4):
                    for ph in range(32):
                        nn = hb * 32 + ph
                        v.wait_ge(pw, 32 + nn + 1)
                        if nn >= 2:
                            v.wait_ge(pfc, nn - 1)
                        v.tensor_copy(
                            out=wfT[nn % 2][:].rearrange(
                                "p (r b c) -> p r b c", r=4, b=2),
                            in_=psum_w[:].rearrange("p (b r c) -> p r b c",
                                                    b=2, r=8)[:, 0:4, :, :])
                        v.drain().then_inc(vw, 1)
                        v.wait_ge(ar, nn + 1)
                        if hb > 0:
                            v.wait_ge(ppr, 4 * hb)
                        v.tensor_tensor(out=hT[:, ph * 512:(ph + 1) * 512],
                                        in0=r_bf[ph % 2][:], in1=r_bf[ph % 2][:],
                                        op=ALU.mult)
                        v.drain().then_inc(vsq, 1)
                    for m in range(4):
                        mt = hb * 4 + m
                        v.wait_ge(ppr, mt + 1)
                        v.tensor_copy(out=yf_sb[:], in_=psum_yf[m % 2][:])
                        yp = yf_sb
                        v.drain()
                        v.tensor_copy(out=mem_f[:], in_=yp[:])
                        v.drain()
                        v.tensor_scalar(out=spk_f[:], in0=mem_f[:], scalar1=1.0,
                                        scalar2=None, op0=ALU.is_gt)
                        v.drain()
                        v.tensor_copy(out=acc_f[:], in_=spk_f[:])
                        v.drain()
                        for _step in range(4):
                            v.scalar_tensor_tensor(out=mem_f[:], in0=mem_f[:],
                                                   scalar=0.9, in1=yp[:],
                                                   op0=ALU.mult, op1=ALU.add)
                            v.drain()
                            v.tensor_tensor(out=mem_f[:], in0=mem_f[:], in1=spk_f[:],
                                            op=ALU.subtract)
                            v.drain()
                            v.tensor_scalar(out=spk_f[:], in0=mem_f[:], scalar1=1.0,
                                            scalar2=None, op0=ALU.is_gt)
                            v.drain()
                            v.tensor_tensor(out=acc_f[:], in0=acc_f[:], in1=spk_f[:],
                                            op=ALU.add)
                            v.drain()
                        v.drain().then_inc(svk, 1)
                        v.wait_ge(pwo, mt + 1)
                        v.wait_ge(sx, 16 * (mt + 1))
                        v.tensor_copy(out=yf_sb[:], in_=psum_yf[m % 2][:])
                        v.drain()
                        v.scalar_tensor_tensor(
                            out=mem_f[:], in0=acc_f[:], scalar=0.2,
                            in1=yf_sb[:], op0=ALU.mult, op1=ALU.add)
                        v.drain()
                        if mt >= 1:
                            v.wait_ge(stt, 16 * mt)
                        v.tensor_tensor(out=o_t[:], in0=mem_f[:], in1=x_t[:],
                                        op=ALU.add)
                        v.drain().then_inc(vres, 1)

            @blkc.sync
            def _(sy):
                sy.wait_ge(bar2, 32)
                for mt in range(16):
                    if mt >= 1:
                        sy.wait_ge(vres, mt)
                        sy.wait_ge(pwo, mt)
                    sy.dma_start(out=x_t[:],
                                 in_=x_own[mt * 128:(mt + 1) * 128, :]).then_inc(sx, 16)
                    sy.dma_start(out=y_ld[:].rearrange("p (h c) -> p h c", h=8),
                                 in_=y_dram[:].rearrange("p (h l) -> p h l", h=8)[
                                     :, :, mt * 128:(mt + 1) * 128]).then_inc(syl, 16)
                    sy.wait_ge(vres, mt + 1)
                    sy.wait_ge(stt, 16 * mt)
                    sy.dma_start(out=out[mt * 128:(mt + 1) * 128, :],
                                 in_=o_t[:]).then_inc(stt, 16)
                sy.wait_ge(stt, 16 * 16)

            blkc.__exit__(None, None, None)

    return nc



def _host_masks():
    k = np.arange(128)[:, None]
    q = np.arange(512)[None, :]
    cols = [((t * 128 + k) <= q).astype(ml_dtypes.bfloat16) for t in range(4)]
    return np.ascontiguousarray(np.concatenate(cols, axis=1))


def kernel(L0_attn_x, L0_attn_wq, L0_attn_wk_down, L0_attn_wv_down,
           L0_attn_wk_up, L0_attn_wv_up, L0_attn_wo, L0_ffn_W_fc, L0_ffn_W_proj):
    x = np.ascontiguousarray(np.asarray(L0_attn_x, dtype=np.float32).reshape(T, D))
    wq_h = np.ascontiguousarray(np.asarray(L0_attn_wq, dtype=np.float32))
    wkv_h = np.ascontiguousarray(np.concatenate(
        [np.asarray(L0_attn_wk_down, dtype=np.float32),
         np.asarray(L0_attn_wv_down, dtype=np.float32)], axis=1))
    wk_up_h = np.ascontiguousarray(np.asarray(L0_attn_wk_up, dtype=np.float32))
    wv_up_h = np.ascontiguousarray(np.asarray(L0_attn_wv_up, dtype=np.float32))
    wo_h = np.ascontiguousarray(np.asarray(L0_attn_wo, dtype=np.float32))
    w_fc_h = np.ascontiguousarray(np.asarray(L0_ffn_W_fc, dtype=np.float32))
    w_proj_h = np.ascontiguousarray(np.asarray(L0_ffn_W_proj, dtype=np.float32))
    masks_h = _host_masks()
    ident_h = np.eye(128, dtype=np.float32)

    nc = build_program()
    in_maps = []
    for c in range(NCORES):
        x_own_h = np.ascontiguousarray(np.concatenate(
            [x[c * 1024:(c + 1) * 1024], x[(15 - c) * 1024:(16 - c) * 1024]], axis=0))
        in_maps.append(dict(x_full=x, x_own=x_own_h, wq=wq_h, wkv=wkv_h,
                            wk_up=wk_up_h, wv_up=wv_up_h, wo=wo_h, w_fc=w_fc_h,
                            w_proj=w_proj_h, masks=masks_h, ident=ident_h))
    res = run_bass_kernel_spmd(nc, in_maps, core_ids=list(range(NCORES)))
    out_full = np.empty((T, D), dtype=np.float32)
    for c in range(NCORES):
        o = res.results[c]["out"]
        out_full[c * 1024:(c + 1) * 1024] = o[0:1024]
        out_full[(15 - c) * 1024:(16 - c) * 1024] = o[1024:2048]
    return out_full.reshape(1, T, D)
